# version 04
# baseline (speedup 1.0000x reference)
import numpy as np
import concourse.bass as bass
import concourse.mybir as mybir
from concourse.bass_utils import run_bass_kernel_spmd
from concourse.alu_op_type import AluOpType

F32 = mybir.dt.float32
BF16 = mybir.dt.bfloat16
I32 = mybir.dt.int32
AF = mybir.ActivationFunctionType
MUL = AluOpType.mult
ADD = AluOpType.add
SUB = AluOpType.subtract

P = 128
D = 2048
T = 1024
TH = 512
NDT = D // P
II = 8192
NIT = II // P
IG = 4
NG = NIT // IG
EPS = 1e-5

C_ZERO, C_ONE, C_INV2048, C_EPS, C_N128 = 0, 1, 2, 3, 4
C_SEL = 5
C_GNW = 8
C_DSC = 24
C_GSC = 40
C_USC = 104
NC_CST = 168

PE, ACT, DVE, GP, SY = "tensor", "scalar", "vector", "gpsimd", "sync"


class Ctx:

    def __init__(self, sems):
        self.sems = sems

    def reset(self, cur, handle):
        self.cur = cur
        self.h = handle
        self.cnt = {k: 0 for k in self.sems}
        self.lastw = {}

    def op(self, eng, fn, incs=()):
        auto = eng in ("vector", "scalar")
        own = "dve" if eng == "vector" else "act"
        if auto:
            self.wait(eng, own, self.cnt[own])
        if self.cur == eng:
            instr = fn(self.h)
            if auto:
                instr.then_inc(self.sems[own], 1)
            for s, k in incs:
                instr.then_inc(self.sems[s], k)
        if auto:
            self.cnt[own] += 1
        for s, k in incs:
            self.cnt[s] += k

    def wait(self, eng, s, val):
        if val <= 0:
            return
        key = (eng, s)
        if self.lastw.get(key, -1) >= val:
            return
        self.lastw[key] = val
        if self.cur == eng:
            self.h.wait_ge(self.sems[s], val)

    def dma(self, eng, s, out_ap, in_ap):
        if self.cur == eng:
            self.h.dma_start(out_ap, in_ap).then_inc(self.sems[s], 16)
        self.cnt[s] += 16
        return self.cnt[s]


def build_program():
    nc = bass.Bass("TRN2", target_bir_lowering=False)

    dr = {}
    dr["xT"] = nc.dram_tensor("xT", [P, NDT, T], F32, kind="ExternalInput").ap()
    for n in ("wiT", "wfT", "wgT"):
        dr[n] = nc.dram_tensor(n, [NDT, P, NDT, P], F32, kind="ExternalInput").ap()
    dr["woT"] = nc.dram_tensor("woT", [8, P, NDT, 256], F32, kind="ExternalInput").ap()
    dr["guT"] = nc.dram_tensor("guT", [32, P, NDT, 256], I32, kind="ExternalInput").ap()
    dr["upT"] = nc.dram_tensor("upT", [32, P, NDT, 256], I32, kind="ExternalInput").ap()
    dr["dnT"] = nc.dram_tensor("dnT", [32, P, 2, D], I32, kind="ExternalInput").ap()
    dr["cstd"] = nc.dram_tensor("cstd", [P, NC_CST], F32, kind="ExternalInput").ap()
    dr["onesd"] = nc.dram_tensor("onesd", [1, P], F32, kind="ExternalInput").ap()
    dr["outT"] = nc.dram_tensor("outT", [D, T], F32, kind="ExternalOutput").ap()
    dr["cc_src0"] = nc.dram_tensor("cc_src0", [P, NDT], F32).ap()
    dr["cc_dst0"] = nc.dram_tensor("cc_dst0", [4 * P, NDT], F32).ap()
    dr["cc_src1"] = nc.dram_tensor("cc_src1", [P, NDT], F32).ap()
    dr["cc_dst1"] = nc.dram_tensor("cc_dst1", [4 * P, NDT], F32).ap()
    dr["aod"] = nc.dram_tensor("aod", [D, T], BF16).ap()

    from contextlib import ExitStack
    es = ExitStack()
    sb = {}

    def S(name, shape, dt):
        sb[name] = es.enter_context(nc.sbuf_tensor(name, shape, dt))
        return sb[name]

    S("xn", [P, NDT, T], BF16)
    S("zeros", [P, T], F32)
    S("bcA", [P, T], F32)
    S("bcB", [P, T], F32)
    S("scrA", [P, 2, TH], F32)
    S("scrB", [P, T], F32)
    S("sgao", [P, 2 * T], BF16)
    S("cst", [P, NC_CST], F32)
    S("onesr", [1, P], F32)
    S("onecb", [P, 1], BF16)
    S("rowv", [1, T], F32)
    S("ends", [P, 2 * NDT], F32)
    S("gst", [P, 3, 2 * NDT], F32)
    S("s012", [P, 3, NDT], F32)
    S("tmp16", [P, 2, NDT], F32)
    S("h0", [P, NDT], F32)
    S("wib", [P, 2, NDT, P], BF16)
    S("wfb", [P, 2, NDT, P], BF16)
    S("AR1", [P, NDT * T], F32)
    S("AR2", [P, NDT * T], F32)

    ar1 = sb["AR1"][:, :]
    ar2 = sb["AR2"][:, :]
    ar1b = ar1.bitcast(BF16)
    ar2b = ar2.bitcast(BF16)
    v = {}
    v["xsb"] = ar1.rearrange("p (a t) -> p a t", a=NDT)
    v["iin"] = v["xsb"]
    v["xbf"] = ar2b[:, 0:2 * T].rearrange("p (a t) -> p a t", a=2)
    v["xsq"] = ar2b[:, 2 * T:4 * T].rearrange("p (a t) -> p a t", a=2)
    v["gate"] = ar2.rearrange("p (a t) -> p a t", a=NDT)
    v["acc"] = v["gate"]
    v["hact"] = ar1b[:, 0:8192].rearrange("p (b a t) -> p b a t", b=2, a=IG)
    v["wgb"] = ar1b[:, 8192:16384].rearrange("p (b a e) -> p b a e", b=2, a=NDT)
    v["wub"] = ar1b[:, 16384:24576].rearrange("p (b a e) -> p b a e", b=2, a=NDT)
    v["dnb"] = ar1b[:, 24576:32768].rearrange("p (b a h) -> p b a h", b=2, a=2)
    v["aosb"] = ar1b[:, 0:NDT * T].rearrange("p (a t) -> p a t", a=NDT)
    v["wob"] = ar1b[:, NDT * T:NDT * T + 8192].rearrange("p (b a e) -> p b a e", b=2, a=NDT)
    v["xeb"] = ar1[:, 12288:14336].rearrange("p (b t) -> p b t", b=2)
    v["gupre"] = ar2b[:, 16384:32768].rearrange("p (c a e) -> p c a e", c=4, a=NDT)
    v["outb"] = ar1[:, 14336:16384].rearrange("p (b t) -> p b t", b=2)

    pb = [es.enter_context(nc.psum_tensor(f"pb{i}", [P, TH], F32)) for i in range(8)]

    sem_names = (["pe", "act", "dve", "ldc", "ldx", "ldx2", "ldg", "ldg2", "ldao", "ldao2", "stcc", "cc",
                  "sao0", "sao1", "sout0", "sout1"]
                 + [f"{r}{b}" for r in ("lwi", "lwf", "lwg", "lwu", "ldn", "lwo", "lxe")
                    for b in (0, 1)])
    sems = {n: es.enter_context(nc.semaphore(n)) for n in sem_names}
    ctx = Ctx(sems)

    block = es.enter_context(nc.Block())

    def prog():
        program(ctx, dr, sb, v, pb)

    @block.tensor
    def _(t):
        ctx.reset(PE, t); prog()

    @block.scalar
    def _(s):
        ctx.reset(ACT, s); prog()

    @block.vector
    def _(ve):
        ctx.reset(DVE, ve); prog()

    @block.gpsimd
    def _(g):
        ctx.reset(GP, g); prog()

    @block.sync
    def _(sy):
        ctx.reset(SY, sy); prog()

    es.close()
    return nc


def program(ctx, dr, sb, v, pb):
    cnt = lambda s: ctx.cnt[s]
    xn, cst = sb["xn"], sb["cst"]
    zeros, bcA, bcB = sb["zeros"], sb["bcA"], sb["bcB"]
    scrA, scrB = sb["scrA"], sb["scrB"]
    scrE = zeros
    gsq = sb["sgao"][:, 0:T].rearrange("p (b t) -> p b t", b=2)
    aosp = sb["sgao"].rearrange("p (b t) -> p b t", b=2)
    sgg = sb["sgao"][:, 0:T].rearrange("p (b t) -> p b t", b=2)
    onesr, onecb = sb["onesr"], sb["onecb"]
    rowv = sb["rowv"]
    ends, gst, s012, tmp16, h0 = sb["ends"], sb["gst"], sb["s012"], sb["tmp16"], sb["h0"]
    wib, wfb = sb["wib"], sb["wfb"]
    xsb, xbf, xsq = v["xsb"], v["xbf"], v["xsq"]
    iin, gate, acc = v["iin"], v["gate"], v["acc"]
    hact, wgb, wub, dnb = v["hact"], v["wgb"], v["wub"], v["dnb"]
    aosb, wob, xeb, outb = v["aosb"], v["wob"], v["xeb"], v["outb"]
    gupre = v["gupre"]

    m = {}

    def wT_chunk(t, k):
        return t[k, :, :, :]

    def gu_src(t, c):
        return t[c, :, :, :]

    def dn_src(c):
        return dr["dnT"][c, :, :, :]

    ctx.dma(GP, "ldc", cst[:, :], dr["cstd"])
    LDC = ctx.dma(GP, "ldc", onesr[:, :], dr["onesd"])
    for k in range(2):
        m[("lwi", k)] = ctx.dma(GP, f"lwi{k}", wib[:, k, :, :], wT_chunk(dr["wiT"], k))
        m[("lwf", k)] = ctx.dma(GP, f"lwf{k}", wfb[:, k, :, :], wT_chunk(dr["wfT"], k))

    ctx.dma(SY, "ldx", xsb[:, 0:8, :], dr["xT"][:, 0:8, :])
    ctx.dma(SY, "ldx2", xsb[:, 8:16, :], dr["xT"][:, 8:16, :])

    ctx.op(DVE, lambda e: e.memset(zeros[:, :], 0.0))
    MS_Z = cnt("dve")
    ctx.wait(ACT, "ldc", LDC)
    ctx.op(ACT, lambda e: e.copy(onecb[:, :], cst[:, C_ONE:C_ONE + 1]))
    MS_OC = cnt("act")

    for dt in range(NDT):
        pr = dt % 2
        ctx.wait(ACT, "ldx" if dt < 8 else "ldx2", 16)
        if dt >= 2:
            ctx.wait(ACT, "pe", m[("stat", dt - 2)])
        ctx.op(ACT, lambda e, dt=dt, pr=pr: e.copy(xbf[:, pr, :], xsb[:, dt, :]))
        ctx.op(ACT, lambda e, dt=dt, pr=pr: e.activation(xsq[:, pr, :], xsb[:, dt, :],
                                                         AF.Square))
        A_XB = cnt("act")
        ctx.wait(PE, "act", MS_OC)
        ctx.wait(PE, "act", A_XB)
        st, sp = dt == 0, dt == NDT - 1
        ctx.op(PE, lambda e, pr=pr, st=st, sp=sp: e.matmul(
            pb[0][0:1, :], onecb[:, 0:1], xbf[:, pr, 0:TH], start=st, stop=sp))
        ctx.op(PE, lambda e, pr=pr, st=st, sp=sp: e.matmul(
            pb[1][0:1, :], onecb[:, 0:1], xbf[:, pr, TH:T], start=st, stop=sp))
        ctx.op(PE, lambda e, pr=pr, st=st, sp=sp: e.matmul(
            pb[2][0:1, :], onecb[:, 0:1], xsq[:, pr, 0:TH], start=st, stop=sp))
        ctx.op(PE, lambda e, pr=pr, st=st, sp=sp: e.matmul(
            pb[3][0:1, :], onecb[:, 0:1], xsq[:, pr, TH:T], start=st, stop=sp),
            incs=[("pe", 1)])
        m[("stat", dt)] = cnt("pe")
    PE_STAT = cnt("pe")

    ctx.wait(DVE, "pe", PE_STAT)
    ctx.wait(DVE, "ldc", LDC)
    for hh in range(2):
        sl = slice(hh * TH, hh * TH + TH)
        ctx.op(DVE, lambda e, sl=sl, hh=hh: e.tensor_scalar_mul(
            bcA[0:1, sl], pb[hh][0:1, :], cst[0:1, C_INV2048:C_INV2048 + 1]))
        ctx.op(DVE, lambda e, sl=sl, hh=hh: e.tensor_scalar_mul(
            rowv[0:1, sl], pb[2 + hh][0:1, :], cst[0:1, C_INV2048:C_INV2048 + 1]))
    ctx.op(DVE, lambda e: e.tensor_tensor(scrB[0:1, :], bcA[0:1, :], bcA[0:1, :], op=MUL))
    ctx.op(DVE, lambda e: e.tensor_tensor(rowv[0:1, :], rowv[0:1, :], scrB[0:1, :], op=SUB))
    DVE_VAR = cnt("dve")
    ctx.wait(ACT, "dve", DVE_VAR)
    ctx.op(ACT, lambda e: e.activation(rowv[0:1, :], rowv[0:1, :], AF.Sqrt,
                                       bias=cst[0:1, C_EPS:C_EPS + 1]))
    A_SD = cnt("act")
    ctx.wait(DVE, "act", A_SD)
    ctx.op(DVE, lambda e: e.reciprocal(bcB[0:1, :], rowv[0:1, :]))
    DVE_RSTD = cnt("dve")

    ctx.wait(PE, "dve", DVE_RSTD)
    for hh in range(2):
        sl = slice(hh * TH, hh * TH + TH)
        ctx.op(PE, lambda e, hh=hh, sl=sl: e.matmul(
            pb[4 + hh][:, :], onesr[0:1, 0:P], bcA[0:1, sl], start=True, stop=True))
        ctx.op(PE, lambda e, hh=hh, sl=sl: e.matmul(
            pb[6 + hh][:, :], onesr[0:1, 0:P], bcB[0:1, sl], start=True, stop=True),
            incs=[("pe", 1)])
    PE_BC = cnt("pe")
    ctx.wait(DVE, "pe", PE_BC)
    for hh in range(2):
        sl = slice(hh * TH, hh * TH + TH)
        ctx.op(DVE, lambda e, hh=hh, sl=sl: e.tensor_copy(bcA[:, sl], pb[4 + hh][:, :]))
        ctx.op(DVE, lambda e, hh=hh, sl=sl: e.tensor_copy(bcB[:, sl], pb[6 + hh][:, :]))
    DVE_BC = cnt("dve")

    for dt in range(NDT):
        ctx.op(DVE, lambda e, dt=dt: e.tensor_tensor(scrB[:, :], xsb[:, dt, :],
                                                     bcA[:, :], op=SUB))
        ctx.op(DVE, lambda e, dt=dt: e.tensor_tensor(xn[:, dt, :], scrB[:, :],
                                                     bcB[:, :], op=MUL))
        m[("xn", dt)] = cnt("dve")
    DVE_XN = cnt("dve")

    for dt in range(NDT):
        ctx.wait(PE, "dve", m[("xn", dt)])
        st, sp = dt == 0, dt == NDT - 1
        ctx.op(PE, lambda e, dt=dt, st=st, sp=sp: e.matmul(
            pb[0][0:1, :], onecb[:, 0:1], xn[:, dt, 0:TH], start=st, stop=sp))
        ctx.op(PE, lambda e, dt=dt, st=st, sp=sp: e.matmul(
            pb[1][0:1, :], onecb[:, 0:1], xn[:, dt, TH:T], start=st, stop=sp),
            incs=[("pe", 1)] if sp else ())
    PE_CS = cnt("pe")
    ctx.wait(DVE, "pe", PE_CS)
    for hh in range(2):
        sl = slice(hh * TH, hh * TH + TH)
        ctx.op(DVE, lambda e, hh=hh, sl=sl: e.tensor_scalar_mul(
            bcB[0:1, sl], pb[hh][0:1, :], cst[0:1, C_N128:C_N128 + 1]))
    DVE_CS = cnt("dve")
    ctx.wait(PE, "dve", DVE_CS)
    for hh in range(2):
        sl = slice(hh * TH, hh * TH + TH)
        ctx.op(PE, lambda e, hh=hh, sl=sl: e.matmul(
            pb[2 + hh][:, :], onesr[0:1, 0:P], bcB[0:1, sl], start=True, stop=True),
            incs=[("pe", 1)])
    PE_CSB = cnt("pe")
    ctx.wait(DVE, "pe", PE_CSB)
    for hh in range(2):
        sl = slice(hh * TH, hh * TH + TH)
        ctx.op(DVE, lambda e, hh=hh, sl=sl: e.tensor_copy(bcB[:, sl], pb[2 + hh][:, :]))
    DVE_CSB = cnt("dve")

    def fixup_half(hf):
        o = NDT * hf
        sA = slice(o, o + NDT, 2)
        sU = slice(o + 1, o + NDT, 2)
        hsl = slice(8 * hf, 8 * hf + 8)
        ctx.wait(DVE, "ldg" if hf == 0 else "ldg2", 16)
        ctx.op(DVE, lambda e, sU=sU, hsl=hsl: e.tensor_copy(
            s012[:, 0, hsl], gst[:, 0, sU]))
        ctx.op(DVE, lambda e, sA=sA, hsl=hsl: e.tensor_tensor(
            tmp16[:, 0, hsl], gst[:, 1, sA], s012[:, 0, hsl], op=MUL))
        ctx.op(DVE, lambda e, sU=sU, hsl=hsl: e.tensor_tensor(
            s012[:, 1, hsl], tmp16[:, 0, hsl], gst[:, 1, sU], op=ADD))
        ctx.op(DVE, lambda e, sA=sA, hsl=hsl: e.tensor_tensor(
            tmp16[:, 1, hsl], gst[:, 2, sA], s012[:, 1, hsl], op=MUL))
        ctx.op(DVE, lambda e, sU=sU, hsl=hsl: e.tensor_tensor(
            s012[:, 2, hsl], tmp16[:, 1, hsl], gst[:, 2, sU], op=ADD))
        ctx.op(DVE, lambda e, hsl=hsl: e.tensor_scalar_mul(
            h0[:, hsl], s012[:, 0, hsl], cst[:, C_SEL:C_SEL + 1]))
        ctx.op(DVE, lambda e, hsl=hsl: e.tensor_scalar_mul(
            tmp16[:, 0, hsl], s012[:, 1, hsl], cst[:, C_SEL + 1:C_SEL + 2]))
        ctx.op(DVE, lambda e, hsl=hsl: e.tensor_scalar_mul(
            tmp16[:, 1, hsl], s012[:, 2, hsl], cst[:, C_SEL + 2:C_SEL + 3]))
        ctx.op(DVE, lambda e, hsl=hsl: e.tensor_tensor(
            h0[:, hsl], h0[:, hsl], tmp16[:, 0, hsl], op=ADD))
        ctx.op(DVE, lambda e, hsl=hsl: e.tensor_tensor(
            h0[:, hsl], h0[:, hsl], tmp16[:, 1, hsl], op=ADD))
        for k in range(8 * hf, 8 * hf + 8):
            ctx.op(DVE, lambda e, k=k: e.scalar_tensor_tensor(
                iin[:, k, :], gate[:, k, :], h0[:, k:k + 1], iin[:, k, :],
                op0=MUL, op1=ADD))
            m[("fix", k)] = cnt("dve")


    for k in range(NDT):
        p4 = 4 * (k % 2)
        b = k % 2
        ctx.wait(PE, f"lwi{b}", m[("lwi", k)])
        ctx.wait(PE, f"lwf{b}", m[("lwf", k)])
        ctx.wait(PE, "dve", DVE_XN)
        if k < 2:
            ctx.wait(PE, "dve", DVE_CSB)
        else:
            ctx.wait(PE, "act", m[("ifree", k - 2)])
        for d in range(NDT):
            st, sp = d == 0, d == NDT - 1
            ctx.op(PE, lambda e, d=d, b=b, p4=p4, st=st, sp=sp: e.matmul(
                pb[p4][:, :], wib[:, b, d, :], xn[:, d, 0:TH], start=st, stop=sp))
            ctx.op(PE, lambda e, d=d, b=b, p4=p4, st=st, sp=sp: e.matmul(
                pb[p4 + 1][:, :], wib[:, b, d, :], xn[:, d, TH:T], start=st, stop=sp),
                incs=[("pe", 1)] if sp else ())
        m[("wif", k)] = cnt("pe")
        for d in range(NDT):
            st, sp = d == 0, d == NDT - 1
            ctx.op(PE, lambda e, d=d, b=b, p4=p4, st=st, sp=sp: e.matmul(
                pb[p4 + 2][:, :], wfb[:, b, d, :], xn[:, d, 0:TH], start=st, stop=sp))
            ctx.op(PE, lambda e, d=d, b=b, p4=p4, st=st, sp=sp: e.matmul(
                pb[p4 + 3][:, :], wfb[:, b, d, :], xn[:, d, TH:T], start=st, stop=sp),
                incs=[("pe", 1)] if sp else ())
        m[("if", k)] = cnt("pe")

        if k + 2 < NDT:
            ctx.wait(GP, "pe", m[("if", k)])
            m[("lwi", k + 2)] = ctx.dma(GP, f"lwi{b}", wib[:, b, :, :],
                                        wT_chunk(dr["wiT"], k + 2))
            m[("lwf", k + 2)] = ctx.dma(GP, f"lwf{b}", wfb[:, b, :, :],
                                        wT_chunk(dr["wfT"], k + 2))

        if k >= 1:
            ctx.wait(ACT, "dve", m[("iin", k - 1)])
        ctx.wait(ACT, "pe", m[("wif", k)])
        ctx.op(ACT, lambda e, p4=p4: e.activation(scrA[:, 0, :], pb[p4][:, :],
                                                  AF.Silu))
        ctx.op(ACT, lambda e, p4=p4: e.activation(scrA[:, 1, :], pb[p4 + 1][:, :],
                                                  AF.Silu))
        ctx.wait(ACT, "pe", m[("if", k)])
        ctx.op(ACT, lambda e, k=k, p4=p4: e.activation(gate[:, k, 0:TH], pb[p4 + 2][:, :],
                                                       AF.Sigmoid))
        ctx.op(ACT, lambda e, k=k, p4=p4: e.activation(gate[:, k, TH:T], pb[p4 + 3][:, :],
                                                       AF.Sigmoid))
        m[("ifree", k)] = cnt("act")

        ctx.wait(DVE, "act", m[("ifree", k)])
        for hh in range(2):
            sl = slice(hh * TH, hh * TH + TH)
            ctx.op(DVE, lambda e, k=k, hh=hh, sl=sl: e.tensor_tensor(
                scrB[:, sl], scrA[:, hh, :], gate[:, k, sl], op=MUL))
            ctx.op(DVE, lambda e, k=k, hh=hh, sl=sl: e.tensor_tensor(
                iin[:, k, sl], scrA[:, hh, :], scrB[:, sl], op=SUB))
        m[("iin", k)] = cnt("dve")
        ctx.wait(DVE, "dve", MS_Z)
        ctx.op(DVE, lambda e, k=k: e.tensor_tensor_scan(
            iin[:, k, :], gate[:, k, :], iin[:, k, :], cst[:, C_ZERO:C_ZERO + 1],
            op0=MUL, op1=ADD))
        ctx.op(DVE, lambda e, k=k: e.tensor_tensor_scan(
            gate[:, k, 0:TH], gate[:, k, 0:TH], zeros[:, 0:TH], cst[:, C_ONE:C_ONE + 1],
            op0=MUL, op1=ADD))
        ctx.op(DVE, lambda e, k=k: e.tensor_tensor_scan(
            gate[:, k, TH:T], gate[:, k, TH:T], zeros[:, 0:TH], gate[:, k, TH - 1:TH],
            op0=MUL, op1=ADD))
        ctx.op(DVE, lambda e, k=k: e.tensor_copy(ends[:, 2 * k:2 * k + 1],
                                                 gate[:, k, T - 1:T]))
        ctx.op(DVE, lambda e, k=k: e.tensor_copy(ends[:, 2 * k + 1:2 * k + 2],
                                                 iin[:, k, T - 1:T]))
        m[("scan", k)] = cnt("dve")
        if k == 11:
            ctx.wait(GP, "cc", 1)
            ctx.dma(GP, "ldg", gst[:, :, 0:NDT],
                    dr["cc_dst0"].rearrange("(r p) e -> p r e", p=P)[:, 0:3, :])
            fixup_half(0)
        if k == 7 or k == NDT - 1:
            hf = 0 if k == 7 else 1
            src = dr["cc_src0"] if hf == 0 else dr["cc_src1"]
            ctx.wait(GP, "dve", m[("scan", k)])
            ctx.dma(GP, "stcc", src, ends[:, NDT * hf:NDT * hf + NDT])
            ctx.wait(GP, "stcc", cnt("stcc"))
            ctx.op(GP, lambda e, src=src, hf=hf: e.collective_compute(
                "AllGather", AluOpType.bypass,
                replica_groups=[[0, 1, 2, 3], [4, 5, 6, 7]],
                ins=[src.opt()],
                outs=[(dr["cc_dst0"] if hf == 0 else dr["cc_dst1"]).opt()]),
                incs=[("cc", 1)])
    DVE_SCANS = cnt("dve")

    ctx.wait(GP, "pe", m[("if", 14)])
    m[("lwc", 0)] = ctx.dma(GP, "lwi0", wib[:, 0, :, :], wT_chunk(dr["wgT"], 0))
    ctx.wait(GP, "pe", m[("if", 15)])
    m[("lwc", 1)] = ctx.dma(GP, "lwi1", wib[:, 1, :, :], wT_chunk(dr["wgT"], 1))

    ctx.wait(GP, "cc", 2)
    ctx.dma(GP, "ldg2", gst[:, :, NDT:2 * NDT],
            dr["cc_dst1"].rearrange("(r p) e -> p r e", p=P)[:, 0:3, :])

    fixup_half(1)

    for k in range(NDT):
        b = k % 2
        p2 = 2 * b
        ctx.wait(PE, f"lwi{b}", m[("lwc", k)])
        if k < 2:
            ctx.wait(PE, "act", m[("ifree", NDT - 1)])
        else:
            ctx.wait(PE, "act", m[("gsq", k - 2)])
            ctx.wait(PE, "dve", m[("tmp", k - 2)])
        for d in range(NDT):
            st, sp = d == 0, d == NDT - 1
            ctx.op(PE, lambda e, d=d, b=b, p2=p2, st=st, sp=sp: e.matmul(
                pb[p2][:, :], wib[:, b, d, :], xn[:, d, 0:TH], start=st, stop=sp))
            ctx.op(PE, lambda e, d=d, b=b, p2=p2, st=st, sp=sp: e.matmul(
                pb[p2 + 1][:, :], wib[:, b, d, :], xn[:, d, TH:T], start=st, stop=sp),
                incs=[("pe", 1)] if sp else ())
        m[("wgc", k)] = cnt("pe")

        if k + 2 < NDT:
            ctx.wait(GP, "pe", m[("wgc", k)])
            m[("lwc", k + 2)] = ctx.dma(GP, f"lwi{b}", wib[:, b, :, :],
                                        wT_chunk(dr["wgT"], k + 2))
        if k == 7:
            ctx.wait(GP, "dve", m[("fix", NDT - 1)])
            m[("lgu", 0)] = (ctx.dma(GP, "lwg0", gupre[:, 0, :, :], gu_src(dr["guT"], 0)),
                             ctx.dma(GP, "lwu0", gupre[:, 1, :, :], gu_src(dr["upT"], 0)))
            m[("lgu", 1)] = (ctx.dma(GP, "lwg1", gupre[:, 2, :, :], gu_src(dr["guT"], 1)),
                             ctx.dma(GP, "lwu1", gupre[:, 3, :, :], gu_src(dr["upT"], 1)))

        ctx.wait(ACT, "pe", m[("wgc", k)])
        if k >= 1:
            ctx.wait(ACT, "pe", m[("ss", k - 1)])
        ctx.op(ACT, lambda e, p2=p2: e.activation(gsq[:, 0, :], pb[p2][:, :],
                                                  AF.Square))
        ctx.op(ACT, lambda e, p2=p2: e.activation(gsq[:, 1, :], pb[p2 + 1][:, :],
                                                  AF.Square))
        m[("gsq", k)] = cnt("act")
        ctx.wait(ACT, "dve", m[("fix", k)])
        if k >= 1:
            ctx.wait(ACT, "dve", m[("tmp", k - 1)])
        for hh in range(2):
            sl = slice(hh * TH, hh * TH + TH)
            ctx.op(ACT, lambda e, k=k, hh=hh, sl=sl: e.activation(
                scrA[:, hh, :], iin[:, k, sl], AF.Sigmoid))
        m[("sgm", k)] = cnt("act")

        ctx.wait(PE, "act", m[("gsq", k)])
        if k == 0:
            ctx.wait(PE, "dve", DVE_BC)
        st, sp = k == 0, k == NDT - 1
        ctx.op(PE, lambda e, st=st, sp=sp: e.matmul(
            pb[4][0:1, :], onecb[:, 0:1], gsq[:, 0, :], start=st, stop=sp))
        ctx.op(PE, lambda e, st=st, sp=sp: e.matmul(
            pb[5][0:1, :], onecb[:, 0:1], gsq[:, 1, :], start=st, stop=sp),
            incs=[("pe", 1)])
        m[("ss", k)] = cnt("pe")

        ctx.wait(DVE, "act", m[("sgm", k)])
        for hh in range(2):
            sl = slice(hh * TH, hh * TH + TH)
            ctx.op(DVE, lambda e, k=k, hh=hh, sl=sl: e.tensor_tensor(
                scrB[:, sl], iin[:, k, sl], scrA[:, hh, :], op=MUL))
        ctx.op(DVE, lambda e, k=k, p2=p2: e.tensor_tensor(
            iin[:, k, 0:TH], pb[p2][:, :], scrB[:, 0:TH], op=MUL))
        ctx.op(DVE, lambda e, k=k, p2=p2: e.tensor_tensor(
            iin[:, k, TH:T], pb[p2 + 1][:, :], scrB[:, TH:T], op=MUL))
        m[("tmp", k)] = cnt("dve")

    ctx.wait(ACT, "pe", m[("ss", NDT - 1)])
    for hh in range(2):
        sl = slice(hh * TH, hh * TH + TH)
        ctx.op(ACT, lambda e, hh=hh, sl=sl: e.activation(
            rowv[0:1, sl], pb[4 + hh][0:1, :], AF.Sqrt,
            scale=cst[0:1, C_INV2048:C_INV2048 + 1], bias=cst[0:1, C_EPS:C_EPS + 1]))
    A_RMS = cnt("act")
    ctx.wait(DVE, "act", A_RMS)
    ctx.op(DVE, lambda e: e.reciprocal(bcA[0:1, :], rowv[0:1, :]))
    DVE_RMS = cnt("dve")
    ctx.wait(PE, "dve", DVE_RMS)
    for hh in range(2):
        sl = slice(hh * TH, hh * TH + TH)
        ctx.op(PE, lambda e, hh=hh, sl=sl: e.matmul(
            pb[6 + hh][:, :], onesr[0:1, 0:P], bcA[0:1, sl], start=True, stop=True),
            incs=[("pe", 1)])
    PE_RMSB = cnt("pe")
    ctx.wait(DVE, "pe", PE_RMSB)
    for hh in range(2):
        sl = slice(hh * TH, hh * TH + TH)
        ctx.op(DVE, lambda e, hh=hh, sl=sl: e.tensor_copy(bcA[:, sl], pb[6 + hh][:, :]))
    DVE_RMSBC = cnt("dve")

    for k in range(NDT):
        b = k % 2
        if k >= 2:
            ctx.wait(DVE, f"sao{b}", 16 * ((k - 2) // 2 + 1))
        ctx.op(DVE, lambda e, k=k, b=b: e.scalar_tensor_tensor(
            aosp[:, b, :], iin[:, k, :], cst[:, C_GNW + k:C_GNW + k + 1], bcA[:, :],
            op0=MUL, op1=MUL))
        m[("ao", k)] = cnt("dve")
        ctx.wait(SY, "dve", m[("ao", k)])
        ctx.dma(SY, f"sao{b}", dr["aod"][k * P:(k + 1) * P, :], aosp[:, b, :])
        m[("sao", k)] = cnt(f"sao{b}")
    DVE_GATE = cnt("dve")

    ctx.wait(GP, "dve", DVE_GATE)
    for c in range(2):
        m[("ldn", c)] = ctx.dma(GP, f"ldn{c}", dnb[:, c, :, :], dn_src(c))

    def g_stat(c, cb, d, sub):
        if c == 0:
            return gupre[:, 0, d, sub:sub + P]
        if c == 1:
            return gupre[:, 2, d, sub:sub + P]
        return wgb[:, cb, d, sub:sub + P]

    def u_stat(c, cb, d, sub):
        if c == 0:
            return gupre[:, 1, d, sub:sub + P]
        if c == 1:
            return gupre[:, 3, d, sub:sub + P]
        return wub[:, cb, d, sub:sub + P]

    def gu_section(g):
        hb = g % 2
        for it in range(IG):
            ti = IG * g + it
            c = ti // 2
            sub = (ti % 2) * P
            cb = c % 2
            ctx.wait(PE, f"lwg{cb}", m[("lgu", c)][0])
            ctx.wait(PE, f"lwu{cb}", m[("lgu", c)][1])
            if ti == 0:
                ctx.wait(PE, "act", A_RMS)
                ctx.wait(PE, "dve", DVE_RMSBC)
                ctx.wait(PE, "dve", m[("tmp", NDT - 1)])
                ctx.wait(PE, "act", m[("gsq", NDT - 1)])
            else:
                ctx.wait(PE, "dve", m[("gsub", ti - 1)])
            for d in range(NDT):
                st, sp = d == 0, d == NDT - 1
                ctx.op(PE, lambda e, d=d, c=c, cb=cb, sub=sub, st=st, sp=sp: e.matmul(
                    pb[0][:, :], g_stat(c, cb, d, sub), xn[:, d, 0:TH],
                    start=st, stop=sp))
                ctx.op(PE, lambda e, d=d, c=c, cb=cb, sub=sub, st=st, sp=sp: e.matmul(
                    pb[1][:, :], g_stat(c, cb, d, sub), xn[:, d, TH:T],
                    start=st, stop=sp), incs=[("pe", 1)] if sp else ())
            m[("gmm", ti)] = cnt("pe")
            if ti >= 1:
                ctx.wait(PE, "dve", m[("hact", ti - 1)])
            for d in range(NDT):
                st, sp = d == 0, d == NDT - 1
                ctx.op(PE, lambda e, d=d, c=c, cb=cb, sub=sub, st=st, sp=sp: e.matmul(
                    pb[2][:, :], u_stat(c, cb, d, sub), xn[:, d, 0:TH],
                    start=st, stop=sp))
                ctx.op(PE, lambda e, d=d, c=c, cb=cb, sub=sub, st=st, sp=sp: e.matmul(
                    pb[3][:, :], u_stat(c, cb, d, sub), xn[:, d, TH:T],
                    start=st, stop=sp), incs=[("pe", 1)] if sp else ())
            m[("umm", ti)] = cnt("pe")
            if ti % 2 == 1:
                m[("guf", c)] = cnt("pe")

            ctx.wait(DVE, "pe", m[("gmm", ti)])
            if g >= 2 and it == 0:
                ctx.wait(DVE, "pe", m[("dng", g - 2)])
            for hh in range(2):
                sl = slice(hh * TH, hh * TH + TH)
                ctx.op(DVE, lambda e, hh=hh, sl=sl: e.tensor_tensor(
                    scrB[:, sl], pb[hh][:, :], bcB[:, sl], op=ADD))
            m[("gsub", ti)] = cnt("dve")
            ctx.wait(ACT, "dve", m[("gsub", ti)])
            if ti == 0:
                ctx.wait(ACT, "sao0", 16 * (NDT // 2))
                ctx.wait(ACT, "sao1", 16 * (NDT // 2))
            if ti >= 1:
                ctx.wait(ACT, "dve", m[("hact", ti - 1)])
            for hh in range(2):
                ctx.op(ACT, lambda e, ti=ti, hh=hh: e.activation(
                    sgg[:, hh, :], scrB[:, hh * TH:hh * TH + TH], AF.Silu,
                    scale=cst[:, C_GSC + ti:C_GSC + ti + 1]))
            m[("sgg", ti)] = cnt("act")
            ctx.wait(DVE, "pe", m[("umm", ti)])
            ctx.wait(DVE, "act", m[("sgg", ti)])
            for hh in range(2):
                sl = slice(hh * TH, hh * TH + TH)
                ctx.op(DVE, lambda e, hh=hh, sl=sl: e.tensor_tensor(
                    scrA[:, hh, :], pb[2 + hh][:, :], bcB[:, sl], op=ADD))
                ctx.op(DVE, lambda e, ti=ti, it=it, hb=hb, hh=hh: e.scalar_tensor_tensor(
                    hact[:, hb, it, hh * TH:hh * TH + TH], scrA[:, hh, :],
                    cst[:, C_USC + ti:C_USC + ti + 1], sgg[:, hh, :], op0=MUL, op1=MUL))
            m[("hact", ti)] = cnt("dve")

    def dn_section(g):
        hb = g % 2
        ctx.wait(PE, "dve", m[("hact", IG * g + IG - 1)])
        for it in range(IG):
            ti = IG * g + it
            st, sp = ti == 0, ti == NIT - 1
            ctx.op(PE, lambda e, it=it, hb=hb, st=st, sp=sp: e.matmul(
                pb[6][0:1, :], onecb[:, 0:1], hact[:, hb, it, 0:TH], start=st, stop=sp))
            ctx.op(PE, lambda e, it=it, hb=hb, st=st, sp=sp: e.matmul(
                pb[7][0:1, :], onecb[:, 0:1], hact[:, hb, it, TH:T], start=st, stop=sp),
                incs=[("pe", 1)] if sp else ())
        ctx.wait(PE, f"ldn{(2 * g) % 2}", m[("ldn", 2 * g)])
        ctx.wait(PE, f"ldn{(2 * g + 1) % 2}", m[("ldn", 2 * g + 1)])
        for ht in range(NDT):
            for hh in range(2):
                j = 2 * ht + hh
                dpb = 4 + (j % 2)
                if g > 0 or j >= 2:
                    key = ("dnadd", g, j - 2) if j >= 2 else ("dnadd", g - 1, 30 + j)
                    ctx.wait(PE, "dve", m[key])
                for it in range(IG):
                    st, sp = it == 0, it == IG - 1
                    c = (IG * g + it) // 2
                    a = it % 2
                    cb = c % 2
                    ctx.op(PE, lambda e, it=it, a=a, cb=cb, ht=ht, hh=hh, hb=hb, dpb=dpb, st=st, sp=sp: e.matmul(
                        pb[dpb][:, :], dnb[:, cb, a, ht * P:(ht + 1) * P],
                        hact[:, hb, it, hh * TH:hh * TH + TH], start=st, stop=sp),
                        incs=[("pe", 1)] if sp else ())
                m[("dnmm", g, j)] = cnt("pe")
                ctx.wait(DVE, "pe", m[("dnmm", g, j)])
                if g == 0:
                    ctx.op(DVE, lambda e, ht=ht, hh=hh, dpb=dpb: e.tensor_copy(
                        acc[:, ht, hh * TH:hh * TH + TH], pb[dpb][:, :]))
                else:
                    ctx.op(DVE, lambda e, ht=ht, hh=hh, dpb=dpb: e.tensor_tensor(
                        acc[:, ht, hh * TH:hh * TH + TH], pb[dpb][:, :],
                        acc[:, ht, hh * TH:hh * TH + TH], op=ADD))
                m[("dnadd", g, j)] = cnt("dve")
        m[("dng", g)] = cnt("pe")

    for g in range(NG):
        gu_section(g)
        for c in (2 * g + 2, 2 * g + 3):
            if c < 2 * NG:
                cb = c % 2
                ctx.wait(GP, "dve", DVE_GATE)
                ctx.wait(GP, "pe", m[("guf", c - 2)])
                m[("lgu", c)] = (ctx.dma(GP, f"lwg{cb}", wgb[:, cb, :, :],
                                         gu_src(dr["guT"], c)),
                                 ctx.dma(GP, f"lwu{cb}", wub[:, cb, :, :],
                                         gu_src(dr["upT"], c)))
        if g >= 1:
            dn_section(g - 1)
            for c in (2 * g, 2 * g + 1):
                cb = c % 2
                ctx.wait(GP, "pe", m[("dng", g - 1)])
                m[("ldn", c)] = ctx.dma(GP, f"ldn{cb}", dnb[:, cb, :, :], dn_src(c))
    ctx.wait(SY, "sao0", m[("sao", NDT - 2)])
    ctx.wait(SY, "sao1", m[("sao", NDT - 1)])
    ctx.wait(SY, "pe", m[("umm", NIT - 1)])
    aodr = dr["aod"].rearrange("(a p) t -> p a t", p=P)
    for j in (2, 3):
        ctx.dma(SY, "ldao", aosb[:, 4 * j:4 * j + 4, :], aodr[:, 4 * j:4 * j + 4, :])
    ctx.wait(SY, "pe", m[("dng", NG - 2)])
    ctx.dma(SY, "ldao", aosb[:, 0:4, :], aodr[:, 0:4, :])
    LDAO = cnt("ldao")
    ctx.wait(GP, "pe", m[("umm", NIT - 1)])
    for c in range(2):
        m[("lwo", c)] = ctx.dma(GP, f"lwo{c}", wob[:, c, :, :], dr["woT"][c, :, :, :])

    dn_section(NG - 1)
    PE_D = cnt("pe")

    ctx.wait(DVE, "pe", PE_D)
    for hh in range(2):
        sl = slice(hh * TH, hh * TH + TH)
        ctx.op(DVE, lambda e, hh=hh, sl=sl: e.tensor_scalar_mul(
            bcA[0:1, sl], pb[6 + hh][0:1, :], cst[0:1, C_N128:C_N128 + 1]))
    DVE_RS = cnt("dve")
    ctx.wait(PE, "dve", DVE_RS)
    for hh in range(2):
        sl = slice(hh * TH, hh * TH + TH)
        ctx.op(PE, lambda e, hh=hh, sl=sl: e.matmul(
            pb[hh][:, :], onesr[0:1, 0:P], bcA[0:1, sl], start=True, stop=True),
            incs=[("pe", 1)])
    PE_RSB = cnt("pe")
    ctx.wait(DVE, "pe", PE_RSB)
    for hh in range(2):
        sl = slice(hh * TH, hh * TH + TH)
        ctx.op(DVE, lambda e, hh=hh, sl=sl: e.tensor_copy(bcA[:, sl], pb[hh][:, :]))
    DVE_RSBC = cnt("dve")

    ctx.wait(SY, "pe", PE_D)
    ctx.dma(SY, "ldao2", aosb[:, 4:8, :], aodr[:, 4:8, :])
    LDAO2 = cnt("ldao2")
    for k in range(2):
        m[("lxe", k)] = ctx.dma(SY, f"lxe{k}", xeb[:, k, :], dr["xT"][:, k, :])

    for k in range(NDT):
        b = k % 2
        c = k // 2
        cb = c % 2
        sub = (k % 2) * P
        ctx.wait(PE, "ldao", LDAO)
        ctx.wait(PE, "ldao2", LDAO2)
        ctx.wait(PE, f"lwo{cb}", m[("lwo", c)])
        if k < 2:
            ctx.wait(PE, "dve", DVE_RSBC)
            ctx.wait(PE, "dve", m[("dnadd", NG - 1, 31)])
        else:
            ctx.wait(PE, "dve", m[("sfree", k - 2)])
        for hh in range(2):
            spb = 2 * b + hh
            for et in range(NDT):
                st, sp = et == 0, et == NDT - 1
                ctx.op(PE, lambda e, et=et, cb=cb, sub=sub, hh=hh, spb=spb, st=st, sp=sp: e.matmul(
                    pb[spb][:, :], wob[:, cb, et, sub:sub + P],
                    aosb[:, et, hh * TH:hh * TH + TH], start=st, stop=sp),
                    incs=[("pe", 1)] if sp else ())
        m[("smm", k)] = cnt("pe")
        if k % 2 == 1:
            m[("wof", c)] = cnt("pe")

        if k >= 1:
            ctx.wait(DVE, "act", m[("gf", k - 1)])
        ctx.wait(DVE, "pe", PE_D)
        ctx.op(DVE, lambda e, k=k: e.tensor_tensor(scrE[:, :], acc[:, k, :],
                                                   bcA[:, :], op=ADD))
        m[("esub", k)] = cnt("dve")
        ctx.wait(ACT, "dve", m[("esub", k)])
        if k >= 1:
            ctx.wait(ACT, "dve", m[("sfree", k - 1)])
        for hh in range(2):
            sl = slice(hh * TH, hh * TH + TH)
            ctx.op(ACT, lambda e, k=k, hh=hh, sl=sl: e.activation(
                scrA[:, hh, :], scrE[:, sl], AF.Silu,
                scale=cst[:, C_DSC + k:C_DSC + k + 1]))
        m[("gf", k)] = cnt("act")
        ctx.wait(DVE, "pe", m[("smm", k)])
        ctx.wait(DVE, "act", m[("gf", k)])
        ctx.wait(DVE, f"lxe{b}", m[("lxe", k)])
        if k >= 2:
            ctx.wait(DVE, f"sout{b}", m[("out", k - 2)])
        for hh in range(2):
            spb = 2 * b + hh
            sl = slice(hh * TH, hh * TH + TH)
            ctx.op(DVE, lambda e, hh=hh, spb=spb, sl=sl: e.tensor_tensor(
                scrB[:, sl], pb[spb][:, :], scrA[:, hh, :], op=MUL))
        m[("sfree", k)] = cnt("dve")
        ctx.op(DVE, lambda e, k=k, b=b: e.tensor_tensor(
            outb[:, b, :], scrB[:, :], xeb[:, b, :], op=ADD))
        m[("outv", k)] = cnt("dve")

        ctx.wait(SY, "dve", m[("outv", k)])
        ctx.dma(SY, f"sout{b}", dr["outT"][k * P:(k + 1) * P, :], outb[:, b, :])
        m[("out", k)] = cnt(f"sout{b}")
        if k + 2 < NDT:
            ctx.wait(SY, "dve", m[("outv", k)])
            m[("lxe", k + 2)] = ctx.dma(SY, f"lxe{b}", xeb[:, b, :],
                                        dr["xT"][:, k + 2, :])
            if (k + 2) % 2 == 0 and (k + 2) // 2 >= 2:
                c2 = (k + 2) // 2
                ctx.wait(GP, "pe", m[("wof", c2 - 2)])
                m[("lwo", c2)] = ctx.dma(GP, f"lwo{c2 % 2}", wob[:, c2 % 2, :, :],
                                         dr["woT"][c2, :, :, :])

    ctx.wait(SY, "sout0", m[("out", NDT - 2)])
    ctx.wait(SY, "sout1", m[("out", NDT - 1)])


_NC_CACHE = None


def make_in_maps(inputs):
    x = np.asarray(inputs["L0_attn_hidden_states"])

    def tile_wT(Wm):
        return np.ascontiguousarray(
            np.asarray(Wm).T.reshape(NDT, P, NDT, P).transpose(2, 1, 0, 3))

    wiT = tile_wT(inputs["L0_attn_Wi"])
    wfT = tile_wT(inputs["L0_attn_Wf"])
    wgT = tile_wT(inputs["L0_attn_Wg"])
    woT = np.ascontiguousarray(
        np.asarray(inputs["L0_attn_Wo"]).T.reshape(NDT, P, 8, 256).transpose(2, 1, 0, 3))
    guT = np.ascontiguousarray(
        np.asarray(inputs["L0_ffn_gate_w"]).T.reshape(NDT, P, 32, 256).transpose(2, 1, 0, 3))
    upT = np.ascontiguousarray(
        np.asarray(inputs["L0_ffn_up_w"]).T.reshape(NDT, P, 32, 256).transpose(2, 1, 0, 3))
    dnT = np.ascontiguousarray(
        np.asarray(inputs["L0_ffn_down_w"]).T.reshape(32, 2, P, D).transpose(0, 2, 1, 3))
    gnw = np.ascontiguousarray(np.asarray(inputs["L0_attn_g_norm_weight"]).reshape(NDT, P).T)
    gsc = np.ascontiguousarray(np.asarray(inputs["L0_ffn_gate_scale"]).reshape(NIT, P).T)
    usc = np.ascontiguousarray(np.asarray(inputs["L0_ffn_up_scale"]).reshape(NIT, P).T)
    dsc = np.ascontiguousarray(np.asarray(inputs["L0_ffn_down_scale"]).reshape(NDT, P).T)
    onesd = np.ones((1, P), np.float32)

    in_maps = []
    for cix in range(8):
        bb, r = cix // 4, cix % 4
        xTc = np.ascontiguousarray(
            x[bb, r * T:(r + 1) * T, :].T.reshape(NDT, P, T).transpose(1, 0, 2))
        cstv = np.zeros((P, NC_CST), np.float32)
        cstv[:, C_ONE] = 1.0
        cstv[:, C_INV2048] = 1.0 / D
        cstv[:, C_EPS] = EPS
        cstv[:, C_N128] = -128.0
        if r > 0:
            cstv[:, C_SEL + r - 1] = 1.0
        cstv[:, C_GNW:C_GNW + NDT] = gnw
        cstv[:, C_DSC:C_DSC + NDT] = dsc
        cstv[:, C_GSC:C_GSC + NIT] = gsc
        cstv[:, C_USC:C_USC + NIT] = usc
        in_maps.append(dict(xT=xTc, wiT=wiT, wfT=wfT, wgT=wgT, woT=woT,
                            guT=guT, upT=upT, dnT=dnT, cstd=cstv, onesd=onesd))
    return in_maps


def kernel(L0_attn_hidden_states, L0_attn_Wi, L0_attn_Wf, L0_attn_Wg,
           L0_attn_g_norm_weight, L0_attn_Wo, L0_ffn_gate_w, L0_ffn_gate_scale,
           L0_ffn_up_w, L0_ffn_up_scale, L0_ffn_down_w, L0_ffn_down_scale):
    global _NC_CACHE
    if _NC_CACHE is None:
        _NC_CACHE = build_program()
    nc = _NC_CACHE
    in_maps = make_in_maps(dict(
        L0_attn_hidden_states=L0_attn_hidden_states, L0_attn_Wi=L0_attn_Wi,
        L0_attn_Wf=L0_attn_Wf, L0_attn_Wg=L0_attn_Wg,
        L0_attn_g_norm_weight=L0_attn_g_norm_weight, L0_attn_Wo=L0_attn_Wo,
        L0_ffn_gate_w=L0_ffn_gate_w, L0_ffn_gate_scale=L0_ffn_gate_scale,
        L0_ffn_up_w=L0_ffn_up_w, L0_ffn_up_scale=L0_ffn_up_scale,
        L0_ffn_down_w=L0_ffn_down_w, L0_ffn_down_scale=L0_ffn_down_scale))

    res = run_bass_kernel_spmd(nc, in_maps, core_ids=list(range(8)))

    out = np.empty((2, 4096, D), np.float32)
    for cix in range(8):
        bb, r = cix // 4, cix % 4
        out[bb, r * T:(r + 1) * T, :] = res.results[cix]["outT"].T
    return out


# version 05
# speedup vs baseline: 1.0423x; 1.0423x over previous
import numpy as np
import concourse.bass as bass
import concourse.mybir as mybir
from concourse.bass_utils import run_bass_kernel_spmd
from concourse.alu_op_type import AluOpType

F32 = mybir.dt.float32
BF16 = mybir.dt.bfloat16
I32 = mybir.dt.int32
AF = mybir.ActivationFunctionType
MUL = AluOpType.mult
ADD = AluOpType.add
SUB = AluOpType.subtract

P = 128
D = 2048
T = 1024
TH = 512
NDT = D // P
II = 8192
NIT = II // P
IG = 4
NG = NIT // IG
EPS = 1e-5

C_ZERO, C_ONE, C_INV2048, C_EPS, C_N128 = 0, 1, 2, 3, 4
C_SEL = 5
C_GNW = 8
C_DSC = 24
C_GSC = 40
C_USC = 104
NC_CST = 168

PE, ACT, DVE, GP, SY = "tensor", "scalar", "vector", "gpsimd", "sync"


class Ctx:

    def __init__(self, sems):
        self.sems = sems

    def reset(self, cur, handle):
        self.cur = cur
        self.h = handle
        self.cnt = {k: 0 for k in self.sems}
        self.lastw = {}

    def op(self, eng, fn, incs=()):
        auto = eng in ("vector", "scalar")
        own = "dve" if eng == "vector" else "act"
        if auto:
            self.wait(eng, own, self.cnt[own])
        if self.cur == eng:
            instr = fn(self.h)
            if auto:
                instr.then_inc(self.sems[own], 1)
            for s, k in incs:
                instr.then_inc(self.sems[s], k)
        if auto:
            self.cnt[own] += 1
        for s, k in incs:
            self.cnt[s] += k

    def wait(self, eng, s, val):
        if val <= 0:
            return
        key = (eng, s)
        if self.lastw.get(key, -1) >= val:
            return
        self.lastw[key] = val
        if self.cur == eng:
            self.h.wait_ge(self.sems[s], val)

    def dma(self, eng, s, out_ap, in_ap):
        if self.cur == eng:
            self.h.dma_start(out_ap, in_ap).then_inc(self.sems[s], 16)
        self.cnt[s] += 16
        return self.cnt[s]


def build_program():
    nc = bass.Bass("TRN2", target_bir_lowering=False)

    dr = {}
    dr["xT"] = nc.dram_tensor("xT", [P, NDT, T], F32, kind="ExternalInput").ap()
    for n in ("wiT", "wfT", "wgT"):
        dr[n] = nc.dram_tensor(n, [NDT, P, NDT, P], F32, kind="ExternalInput").ap()
    dr["woT"] = nc.dram_tensor("woT", [8, P, NDT, 256], F32, kind="ExternalInput").ap()
    dr["guT"] = nc.dram_tensor("guT", [32, P, NDT, 256], I32, kind="ExternalInput").ap()
    dr["upT"] = nc.dram_tensor("upT", [32, P, NDT, 256], I32, kind="ExternalInput").ap()
    dr["dnT"] = nc.dram_tensor("dnT", [32, P, 2, D], I32, kind="ExternalInput").ap()
    dr["cstd"] = nc.dram_tensor("cstd", [P, NC_CST], F32, kind="ExternalInput").ap()
    dr["onesd"] = nc.dram_tensor("onesd", [1, P], F32, kind="ExternalInput").ap()
    dr["outT"] = nc.dram_tensor("outT", [D, T], F32, kind="ExternalOutput").ap()
    dr["cc_src0"] = nc.dram_tensor("cc_src0", [P, NDT], F32).ap()
    dr["cc_dst0"] = nc.dram_tensor("cc_dst0", [4 * P, NDT], F32).ap()
    dr["cc_src1"] = nc.dram_tensor("cc_src1", [P, NDT], F32).ap()
    dr["cc_dst1"] = nc.dram_tensor("cc_dst1", [4 * P, NDT], F32).ap()
    dr["aod"] = nc.dram_tensor("aod", [D, T], BF16).ap()

    from contextlib import ExitStack
    es = ExitStack()
    sb = {}

    def S(name, shape, dt):
        sb[name] = es.enter_context(nc.sbuf_tensor(name, shape, dt))
        return sb[name]

    S("xn", [P, NDT, T], BF16)
    S("zeros", [P, T], F32)
    S("bcA", [P, T], F32)
    S("bcB", [P, T], F32)
    S("scrA", [P, 2, TH], F32)
    S("scrB", [P, T], F32)
    S("sgao", [P, 2 * T], BF16)
    S("cst", [P, NC_CST], F32)
    S("onesr", [1, P], F32)
    S("onecb", [P, 1], BF16)
    S("rowv", [1, T], F32)
    S("ends", [P, 2 * NDT], F32)
    S("gst", [P, 3, 2 * NDT], F32)
    S("s012", [P, 3, NDT], F32)
    S("tmp16", [P, 2, NDT], F32)
    S("h0", [P, NDT], F32)
    S("wib", [P, 2, NDT, P], BF16)
    S("wfb", [P, 2, NDT, P], BF16)
    S("AR1", [P, NDT * T], F32)
    S("AR2", [P, NDT * T], F32)

    ar1 = sb["AR1"][:, :]
    ar2 = sb["AR2"][:, :]
    ar1b = ar1.bitcast(BF16)
    ar2b = ar2.bitcast(BF16)
    v = {}
    v["xsb"] = ar1.rearrange("p (a t) -> p a t", a=NDT)
    v["iin"] = v["xsb"]
    v["xbf"] = ar2b[:, 0:2 * T].rearrange("p (a t) -> p a t", a=2)
    v["xsq"] = ar2b[:, 2 * T:4 * T].rearrange("p (a t) -> p a t", a=2)
    v["gate"] = ar2.rearrange("p (a t) -> p a t", a=NDT)
    v["acc"] = v["gate"]
    v["hact"] = ar1b[:, 0:8192].rearrange("p (b a t) -> p b a t", b=2, a=IG)
    v["wgb"] = ar1b[:, 8192:16384].rearrange("p (b a e) -> p b a e", b=2, a=NDT)
    v["wub"] = ar1b[:, 16384:24576].rearrange("p (b a e) -> p b a e", b=2, a=NDT)
    v["dnb"] = ar1b[:, 24576:32768].rearrange("p (b a h) -> p b a h", b=2, a=2)
    v["aosb"] = ar1b[:, 0:NDT * T].rearrange("p (a t) -> p a t", a=NDT)
    v["wob"] = ar1b[:, NDT * T:NDT * T + 8192].rearrange("p (b a e) -> p b a e", b=2, a=NDT)
    v["xeb"] = ar1[:, 12288:14336].rearrange("p (b t) -> p b t", b=2)
    v["gupre"] = ar2b[:, 16384:32768].rearrange("p (c a e) -> p c a e", c=4, a=NDT)
    v["outb"] = ar1[:, 14336:16384].rearrange("p (b t) -> p b t", b=2)

    pb = [es.enter_context(nc.psum_tensor(f"pb{i}", [P, TH], F32)) for i in range(8)]

    sem_names = (["pe", "act", "dve", "ldc", "ldx", "ldx2", "ldg", "ldg2", "ldao", "ldao2", "ldx3", "ldx4", "stcc", "cc",
                  "sao0", "sao1", "sout0", "sout1"]
                 + [f"{r}{b}" for r in ("lwi", "lwf", "lwg", "lwu", "ldn", "lwo", "lxe")
                    for b in (0, 1)])
    sems = {n: es.enter_context(nc.semaphore(n)) for n in sem_names}
    ctx = Ctx(sems)

    block = es.enter_context(nc.Block())

    def prog():
        program(ctx, dr, sb, v, pb)

    @block.tensor
    def _(t):
        ctx.reset(PE, t); prog()

    @block.scalar
    def _(s):
        ctx.reset(ACT, s); prog()

    @block.vector
    def _(ve):
        ctx.reset(DVE, ve); prog()

    @block.gpsimd
    def _(g):
        ctx.reset(GP, g); prog()

    @block.sync
    def _(sy):
        ctx.reset(SY, sy); prog()

    es.close()
    return nc


def program(ctx, dr, sb, v, pb):
    cnt = lambda s: ctx.cnt[s]
    xn, cst = sb["xn"], sb["cst"]
    zeros, bcA, bcB = sb["zeros"], sb["bcA"], sb["bcB"]
    scrA, scrB = sb["scrA"], sb["scrB"]
    scrE = zeros
    gsq = sb["sgao"][:, 0:T].rearrange("p (b t) -> p b t", b=2)
    aosp = sb["sgao"].rearrange("p (b t) -> p b t", b=2)
    sgg = sb["sgao"][:, 0:T].rearrange("p (b t) -> p b t", b=2)
    onesr, onecb = sb["onesr"], sb["onecb"]
    rowv = sb["rowv"]
    ends, gst, s012, tmp16, h0 = sb["ends"], sb["gst"], sb["s012"], sb["tmp16"], sb["h0"]
    wib, wfb = sb["wib"], sb["wfb"]
    xsb, xbf, xsq = v["xsb"], v["xbf"], v["xsq"]
    iin, gate, acc = v["iin"], v["gate"], v["acc"]
    hact, wgb, wub, dnb = v["hact"], v["wgb"], v["wub"], v["dnb"]
    aosb, wob, xeb, outb = v["aosb"], v["wob"], v["xeb"], v["outb"]
    gupre = v["gupre"]

    m = {}

    def wT_chunk(t, k):
        return t[k, :, :, :]

    def gu_src(t, c):
        return t[c, :, :, :]

    def dn_src(c):
        return dr["dnT"][c, :, :, :]

    ctx.dma(GP, "ldc", cst[:, :], dr["cstd"])
    LDC = ctx.dma(GP, "ldc", onesr[:, :], dr["onesd"])
    for k in range(2):
        m[("lwi", k)] = ctx.dma(GP, f"lwi{k}", wib[:, k, :, :], wT_chunk(dr["wiT"], k))
        m[("lwf", k)] = ctx.dma(GP, f"lwf{k}", wfb[:, k, :, :], wT_chunk(dr["wfT"], k))

    for j, sm in enumerate(("ldx", "ldx2", "ldx3", "ldx4")):
        ctx.dma(SY, sm, xsb[:, 4 * j:4 * j + 4, :], dr["xT"][:, 4 * j:4 * j + 4, :])

    ctx.op(DVE, lambda e: e.memset(zeros[:, :], 0.0))
    MS_Z = cnt("dve")
    ctx.wait(ACT, "ldc", LDC)
    ctx.op(ACT, lambda e: e.copy(onecb[:, :], cst[:, C_ONE:C_ONE + 1]))
    MS_OC = cnt("act")

    for dt in range(NDT):
        pr = dt % 2
        ctx.wait(ACT, ("ldx", "ldx2", "ldx3", "ldx4")[dt // 4], 16)
        if dt >= 2:
            ctx.wait(ACT, "pe", m[("stat", dt - 2)])
        ctx.wait(DVE, ("ldx", "ldx2", "ldx3", "ldx4")[dt // 4], 16)
        if dt >= 2:
            ctx.wait(DVE, "pe", m[("stat", dt - 2)])
        ctx.op(DVE, lambda e, dt=dt, pr=pr: e.tensor_copy(xbf[:, pr, :], xsb[:, dt, :]))
        D_XB = cnt("dve")
        ctx.op(ACT, lambda e, dt=dt, pr=pr: e.activation(xsq[:, pr, :], xsb[:, dt, :],
                                                         AF.Square))
        A_XB = cnt("act")
        ctx.wait(PE, "act", MS_OC)
        ctx.wait(PE, "dve", D_XB)
        ctx.wait(PE, "act", A_XB)
        st, sp = dt == 0, dt == NDT - 1
        ctx.op(PE, lambda e, pr=pr, st=st, sp=sp: e.matmul(
            pb[0][0:1, :], onecb[:, 0:1], xbf[:, pr, 0:TH], start=st, stop=sp))
        ctx.op(PE, lambda e, pr=pr, st=st, sp=sp: e.matmul(
            pb[1][0:1, :], onecb[:, 0:1], xbf[:, pr, TH:T], start=st, stop=sp))
        ctx.op(PE, lambda e, pr=pr, st=st, sp=sp: e.matmul(
            pb[2][0:1, :], onecb[:, 0:1], xsq[:, pr, 0:TH], start=st, stop=sp))
        ctx.op(PE, lambda e, pr=pr, st=st, sp=sp: e.matmul(
            pb[3][0:1, :], onecb[:, 0:1], xsq[:, pr, TH:T], start=st, stop=sp),
            incs=[("pe", 1)])
        m[("stat", dt)] = cnt("pe")
    PE_STAT = cnt("pe")

    ctx.wait(DVE, "pe", PE_STAT)
    ctx.wait(DVE, "ldc", LDC)
    for hh in range(2):
        sl = slice(hh * TH, hh * TH + TH)
        ctx.op(DVE, lambda e, sl=sl, hh=hh: e.tensor_scalar_mul(
            bcA[0:1, sl], pb[hh][0:1, :], cst[0:1, C_INV2048:C_INV2048 + 1]))
        ctx.op(DVE, lambda e, sl=sl, hh=hh: e.tensor_scalar_mul(
            rowv[0:1, sl], pb[2 + hh][0:1, :], cst[0:1, C_INV2048:C_INV2048 + 1]))
    ctx.op(DVE, lambda e: e.tensor_tensor(scrB[0:1, :], bcA[0:1, :], bcA[0:1, :], op=MUL))
    ctx.op(DVE, lambda e: e.tensor_tensor(rowv[0:1, :], rowv[0:1, :], scrB[0:1, :], op=SUB))
    DVE_VAR = cnt("dve")
    ctx.wait(ACT, "dve", DVE_VAR)
    ctx.op(ACT, lambda e: e.activation(rowv[0:1, :], rowv[0:1, :], AF.Sqrt,
                                       bias=cst[0:1, C_EPS:C_EPS + 1]))
    A_SD = cnt("act")
    ctx.wait(DVE, "act", A_SD)
    ctx.op(DVE, lambda e: e.reciprocal(bcB[0:1, :], rowv[0:1, :]))
    DVE_RSTD = cnt("dve")

    ctx.wait(PE, "dve", DVE_RSTD)
    for hh in range(2):
        sl = slice(hh * TH, hh * TH + TH)
        ctx.op(PE, lambda e, hh=hh, sl=sl: e.matmul(
            pb[4 + hh][:, :], onesr[0:1, 0:P], bcA[0:1, sl], start=True, stop=True))
        ctx.op(PE, lambda e, hh=hh, sl=sl: e.matmul(
            pb[6 + hh][:, :], onesr[0:1, 0:P], bcB[0:1, sl], start=True, stop=True),
            incs=[("pe", 1)])
    PE_BC = cnt("pe")
    ctx.wait(DVE, "pe", PE_BC)
    for hh in range(2):
        sl = slice(hh * TH, hh * TH + TH)
        ctx.op(DVE, lambda e, hh=hh, sl=sl: e.tensor_copy(bcA[:, sl], pb[4 + hh][:, :]))
        ctx.op(DVE, lambda e, hh=hh, sl=sl: e.tensor_copy(bcB[:, sl], pb[6 + hh][:, :]))
    DVE_BC = cnt("dve")

    for dt in range(NDT):
        ctx.op(DVE, lambda e, dt=dt: e.tensor_tensor(scrB[:, :], xsb[:, dt, :],
                                                     bcA[:, :], op=SUB))
        ctx.op(DVE, lambda e, dt=dt: e.tensor_tensor(xn[:, dt, :], scrB[:, :],
                                                     bcB[:, :], op=MUL))
        m[("xn", dt)] = cnt("dve")
    DVE_XN = cnt("dve")

    for dt in range(NDT):
        ctx.wait(PE, "dve", m[("xn", dt)])
        st, sp = dt == 0, dt == NDT - 1
        ctx.op(PE, lambda e, dt=dt, st=st, sp=sp: e.matmul(
            pb[0][0:1, :], onecb[:, 0:1], xn[:, dt, 0:TH], start=st, stop=sp))
        ctx.op(PE, lambda e, dt=dt, st=st, sp=sp: e.matmul(
            pb[1][0:1, :], onecb[:, 0:1], xn[:, dt, TH:T], start=st, stop=sp),
            incs=[("pe", 1)] if sp else ())
    PE_CS = cnt("pe")
    ctx.wait(DVE, "pe", PE_CS)
    for hh in range(2):
        sl = slice(hh * TH, hh * TH + TH)
        ctx.op(DVE, lambda e, hh=hh, sl=sl: e.tensor_scalar_mul(
            bcB[0:1, sl], pb[hh][0:1, :], cst[0:1, C_N128:C_N128 + 1]))
    DVE_CS = cnt("dve")
    ctx.wait(PE, "dve", DVE_CS)
    for hh in range(2):
        sl = slice(hh * TH, hh * TH + TH)
        ctx.op(PE, lambda e, hh=hh, sl=sl: e.matmul(
            pb[2 + hh][:, :], onesr[0:1, 0:P], bcB[0:1, sl], start=True, stop=True),
            incs=[("pe", 1)])
    PE_CSB = cnt("pe")
    ctx.wait(DVE, "pe", PE_CSB)
    for hh in range(2):
        sl = slice(hh * TH, hh * TH + TH)
        ctx.op(DVE, lambda e, hh=hh, sl=sl: e.tensor_copy(bcB[:, sl], pb[2 + hh][:, :]))
    DVE_CSB = cnt("dve")

    def fixup_half(hf):
        o = NDT * hf
        sA = slice(o, o + NDT, 2)
        sU = slice(o + 1, o + NDT, 2)
        hsl = slice(8 * hf, 8 * hf + 8)
        ctx.wait(DVE, "ldg" if hf == 0 else "ldg2", 16)
        ctx.op(DVE, lambda e, sU=sU, hsl=hsl: e.tensor_copy(
            s012[:, 0, hsl], gst[:, 0, sU]))
        ctx.op(DVE, lambda e, sA=sA, hsl=hsl: e.tensor_tensor(
            tmp16[:, 0, hsl], gst[:, 1, sA], s012[:, 0, hsl], op=MUL))
        ctx.op(DVE, lambda e, sU=sU, hsl=hsl: e.tensor_tensor(
            s012[:, 1, hsl], tmp16[:, 0, hsl], gst[:, 1, sU], op=ADD))
        ctx.op(DVE, lambda e, sA=sA, hsl=hsl: e.tensor_tensor(
            tmp16[:, 1, hsl], gst[:, 2, sA], s012[:, 1, hsl], op=MUL))
        ctx.op(DVE, lambda e, sU=sU, hsl=hsl: e.tensor_tensor(
            s012[:, 2, hsl], tmp16[:, 1, hsl], gst[:, 2, sU], op=ADD))
        ctx.op(DVE, lambda e, hsl=hsl: e.tensor_scalar_mul(
            h0[:, hsl], s012[:, 0, hsl], cst[:, C_SEL:C_SEL + 1]))
        ctx.op(DVE, lambda e, hsl=hsl: e.tensor_scalar_mul(
            tmp16[:, 0, hsl], s012[:, 1, hsl], cst[:, C_SEL + 1:C_SEL + 2]))
        ctx.op(DVE, lambda e, hsl=hsl: e.tensor_scalar_mul(
            tmp16[:, 1, hsl], s012[:, 2, hsl], cst[:, C_SEL + 2:C_SEL + 3]))
        ctx.op(DVE, lambda e, hsl=hsl: e.tensor_tensor(
            h0[:, hsl], h0[:, hsl], tmp16[:, 0, hsl], op=ADD))
        ctx.op(DVE, lambda e, hsl=hsl: e.tensor_tensor(
            h0[:, hsl], h0[:, hsl], tmp16[:, 1, hsl], op=ADD))
        for k in range(8 * hf, 8 * hf + 8):
            ctx.op(DVE, lambda e, k=k: e.scalar_tensor_tensor(
                iin[:, k, :], gate[:, k, :], h0[:, k:k + 1], iin[:, k, :],
                op0=MUL, op1=ADD))
            m[("fix", k)] = cnt("dve")


    for k in range(NDT):
        p4 = 4 * (k % 2)
        b = k % 2
        ctx.wait(PE, f"lwi{b}", m[("lwi", k)])
        ctx.wait(PE, f"lwf{b}", m[("lwf", k)])
        ctx.wait(PE, "dve", DVE_XN)
        if k < 2:
            ctx.wait(PE, "dve", DVE_CSB)
        else:
            ctx.wait(PE, "act", m[("ifree", k - 2)])
        for d in range(NDT):
            st, sp = d == 0, d == NDT - 1
            ctx.op(PE, lambda e, d=d, b=b, p4=p4, st=st, sp=sp: e.matmul(
                pb[p4][:, :], wib[:, b, d, :], xn[:, d, 0:TH], start=st, stop=sp))
            ctx.op(PE, lambda e, d=d, b=b, p4=p4, st=st, sp=sp: e.matmul(
                pb[p4 + 1][:, :], wib[:, b, d, :], xn[:, d, TH:T], start=st, stop=sp),
                incs=[("pe", 1)] if sp else ())
        m[("wif", k)] = cnt("pe")
        for d in range(NDT):
            st, sp = d == 0, d == NDT - 1
            ctx.op(PE, lambda e, d=d, b=b, p4=p4, st=st, sp=sp: e.matmul(
                pb[p4 + 2][:, :], wfb[:, b, d, :], xn[:, d, 0:TH], start=st, stop=sp))
            ctx.op(PE, lambda e, d=d, b=b, p4=p4, st=st, sp=sp: e.matmul(
                pb[p4 + 3][:, :], wfb[:, b, d, :], xn[:, d, TH:T], start=st, stop=sp),
                incs=[("pe", 1)] if sp else ())
        m[("if", k)] = cnt("pe")

        if k + 2 < NDT:
            ctx.wait(GP, "pe", m[("if", k)])
            m[("lwi", k + 2)] = ctx.dma(GP, f"lwi{b}", wib[:, b, :, :],
                                        wT_chunk(dr["wiT"], k + 2))
            m[("lwf", k + 2)] = ctx.dma(GP, f"lwf{b}", wfb[:, b, :, :],
                                        wT_chunk(dr["wfT"], k + 2))

        if k >= 1:
            ctx.wait(ACT, "dve", m[("iin", k - 1)])
        ctx.wait(ACT, "pe", m[("wif", k)])
        ctx.op(ACT, lambda e, p4=p4: e.activation(scrA[:, 0, :], pb[p4][:, :],
                                                  AF.Silu))
        ctx.op(ACT, lambda e, p4=p4: e.activation(scrA[:, 1, :], pb[p4 + 1][:, :],
                                                  AF.Silu))
        ctx.wait(ACT, "pe", m[("if", k)])
        ctx.op(ACT, lambda e, k=k, p4=p4: e.activation(gate[:, k, 0:TH], pb[p4 + 2][:, :],
                                                       AF.Sigmoid))
        ctx.op(ACT, lambda e, k=k, p4=p4: e.activation(gate[:, k, TH:T], pb[p4 + 3][:, :],
                                                       AF.Sigmoid))
        m[("ifree", k)] = cnt("act")

        ctx.wait(DVE, "act", m[("ifree", k)])
        for hh in range(2):
            sl = slice(hh * TH, hh * TH + TH)
            ctx.op(DVE, lambda e, k=k, hh=hh, sl=sl: e.tensor_tensor(
                scrB[:, sl], scrA[:, hh, :], gate[:, k, sl], op=MUL))
            ctx.op(DVE, lambda e, k=k, hh=hh, sl=sl: e.tensor_tensor(
                iin[:, k, sl], scrA[:, hh, :], scrB[:, sl], op=SUB))
        m[("iin", k)] = cnt("dve")
        ctx.wait(DVE, "dve", MS_Z)
        ctx.op(DVE, lambda e, k=k: e.tensor_tensor_scan(
            iin[:, k, :], gate[:, k, :], iin[:, k, :], cst[:, C_ZERO:C_ZERO + 1],
            op0=MUL, op1=ADD))
        ctx.op(DVE, lambda e, k=k: e.tensor_tensor_scan(
            gate[:, k, 0:TH], gate[:, k, 0:TH], zeros[:, 0:TH], cst[:, C_ONE:C_ONE + 1],
            op0=MUL, op1=ADD))
        ctx.op(DVE, lambda e, k=k: e.tensor_tensor_scan(
            gate[:, k, TH:T], gate[:, k, TH:T], zeros[:, 0:TH], gate[:, k, TH - 1:TH],
            op0=MUL, op1=ADD))
        ctx.op(DVE, lambda e, k=k: e.tensor_copy(ends[:, 2 * k:2 * k + 1],
                                                 gate[:, k, T - 1:T]))
        ctx.op(DVE, lambda e, k=k: e.tensor_copy(ends[:, 2 * k + 1:2 * k + 2],
                                                 iin[:, k, T - 1:T]))
        m[("scan", k)] = cnt("dve")
        if k == 11:
            ctx.wait(GP, "cc", 1)
            ctx.dma(GP, "ldg", gst[:, :, 0:NDT],
                    dr["cc_dst0"].rearrange("(r p) e -> p r e", p=P)[:, 0:3, :])
            fixup_half(0)
        if k == 7 or k == NDT - 1:
            hf = 0 if k == 7 else 1
            src = dr["cc_src0"] if hf == 0 else dr["cc_src1"]
            ctx.wait(GP, "dve", m[("scan", k)])
            ctx.dma(GP, "stcc", src, ends[:, NDT * hf:NDT * hf + NDT])
            ctx.wait(GP, "stcc", cnt("stcc"))
            ctx.op(GP, lambda e, src=src, hf=hf: e.collective_compute(
                "AllGather", AluOpType.bypass,
                replica_groups=[[0, 1, 2, 3], [4, 5, 6, 7]],
                ins=[src.opt()],
                outs=[(dr["cc_dst0"] if hf == 0 else dr["cc_dst1"]).opt()]),
                incs=[("cc", 1)])
    DVE_SCANS = cnt("dve")

    ctx.wait(GP, "pe", m[("if", 14)])
    m[("lwc", 0)] = ctx.dma(GP, "lwi0", wib[:, 0, :, :], wT_chunk(dr["wgT"], 0))
    ctx.wait(GP, "pe", m[("if", 15)])
    m[("lwc", 1)] = ctx.dma(GP, "lwi1", wib[:, 1, :, :], wT_chunk(dr["wgT"], 1))

    ctx.wait(GP, "cc", 2)
    ctx.dma(GP, "ldg2", gst[:, :, NDT:2 * NDT],
            dr["cc_dst1"].rearrange("(r p) e -> p r e", p=P)[:, 0:3, :])

    fixup_half(1)

    for k in range(NDT):
        b = k % 2
        p2 = 2 * b
        ctx.wait(PE, f"lwi{b}", m[("lwc", k)])
        if k < 2:
            ctx.wait(PE, "act", m[("ifree", NDT - 2)])
        else:
            ctx.wait(PE, "act", m[("gsq", k - 2)])
            ctx.wait(PE, "dve", m[("tmp", k - 2)])
        for d in range(NDT):
            st, sp = d == 0, d == NDT - 1
            ctx.op(PE, lambda e, d=d, b=b, p2=p2, st=st, sp=sp: e.matmul(
                pb[p2][:, :], wib[:, b, d, :], xn[:, d, 0:TH], start=st, stop=sp))
            ctx.op(PE, lambda e, d=d, b=b, p2=p2, st=st, sp=sp: e.matmul(
                pb[p2 + 1][:, :], wib[:, b, d, :], xn[:, d, TH:T], start=st, stop=sp),
                incs=[("pe", 1)] if sp else ())
        m[("wgc", k)] = cnt("pe")

        if k + 2 < NDT:
            ctx.wait(GP, "pe", m[("wgc", k)])
            m[("lwc", k + 2)] = ctx.dma(GP, f"lwi{b}", wib[:, b, :, :],
                                        wT_chunk(dr["wgT"], k + 2))
        if k == 7:
            ctx.wait(GP, "dve", m[("fix", NDT - 1)])
            m[("lgu", 0)] = (ctx.dma(GP, "lwg0", gupre[:, 0, :, :], gu_src(dr["guT"], 0)),
                             ctx.dma(GP, "lwu0", gupre[:, 1, :, :], gu_src(dr["upT"], 0)))
            m[("lgu", 1)] = (ctx.dma(GP, "lwg1", gupre[:, 2, :, :], gu_src(dr["guT"], 1)),
                             ctx.dma(GP, "lwu1", gupre[:, 3, :, :], gu_src(dr["upT"], 1)))

        ctx.wait(ACT, "pe", m[("wgc", k)])
        if k >= 1:
            ctx.wait(ACT, "pe", m[("ss", k - 1)])
        ctx.op(ACT, lambda e, p2=p2: e.activation(gsq[:, 0, :], pb[p2][:, :],
                                                  AF.Square))
        ctx.op(ACT, lambda e, p2=p2: e.activation(gsq[:, 1, :], pb[p2 + 1][:, :],
                                                  AF.Square))
        m[("gsq", k)] = cnt("act")
        ctx.wait(ACT, "dve", m[("fix", k)])
        if k >= 1:
            ctx.wait(ACT, "dve", m[("tmp", k - 1)])
        for hh in range(2):
            sl = slice(hh * TH, hh * TH + TH)
            ctx.op(ACT, lambda e, k=k, hh=hh, sl=sl: e.activation(
                scrA[:, hh, :], iin[:, k, sl], AF.Sigmoid))
        m[("sgm", k)] = cnt("act")

        ctx.wait(PE, "act", m[("gsq", k)])
        if k == 0:
            ctx.wait(PE, "dve", DVE_BC)
            ctx.wait(PE, "act", m[("ifree", NDT - 1)])
        st, sp = k == 0, k == NDT - 1
        ctx.op(PE, lambda e, st=st, sp=sp: e.matmul(
            pb[4][0:1, :], onecb[:, 0:1], gsq[:, 0, :], start=st, stop=sp))
        ctx.op(PE, lambda e, st=st, sp=sp: e.matmul(
            pb[5][0:1, :], onecb[:, 0:1], gsq[:, 1, :], start=st, stop=sp),
            incs=[("pe", 1)])
        m[("ss", k)] = cnt("pe")

        ctx.wait(DVE, "act", m[("sgm", k)])
        for hh in range(2):
            sl = slice(hh * TH, hh * TH + TH)
            ctx.op(DVE, lambda e, k=k, hh=hh, sl=sl: e.tensor_tensor(
                scrB[:, sl], iin[:, k, sl], scrA[:, hh, :], op=MUL))
        ctx.op(DVE, lambda e, k=k, p2=p2: e.tensor_tensor(
            iin[:, k, 0:TH], pb[p2][:, :], scrB[:, 0:TH], op=MUL))
        ctx.op(DVE, lambda e, k=k, p2=p2: e.tensor_tensor(
            iin[:, k, TH:T], pb[p2 + 1][:, :], scrB[:, TH:T], op=MUL))
        m[("tmp", k)] = cnt("dve")

    ctx.wait(ACT, "pe", m[("ss", NDT - 1)])
    for hh in range(2):
        sl = slice(hh * TH, hh * TH + TH)
        ctx.op(ACT, lambda e, hh=hh, sl=sl: e.activation(
            rowv[0:1, sl], pb[4 + hh][0:1, :], AF.Sqrt,
            scale=cst[0:1, C_INV2048:C_INV2048 + 1], bias=cst[0:1, C_EPS:C_EPS + 1]))
    A_RMS = cnt("act")
    ctx.wait(DVE, "act", A_RMS)
    ctx.op(DVE, lambda e: e.reciprocal(bcA[0:1, :], rowv[0:1, :]))
    DVE_RMS = cnt("dve")
    ctx.wait(PE, "dve", DVE_RMS)
    for hh in range(2):
        sl = slice(hh * TH, hh * TH + TH)
        ctx.op(PE, lambda e, hh=hh, sl=sl: e.matmul(
            pb[6 + hh][:, :], onesr[0:1, 0:P], bcA[0:1, sl], start=True, stop=True),
            incs=[("pe", 1)])
    PE_RMSB = cnt("pe")
    ctx.wait(DVE, "pe", PE_RMSB)
    for hh in range(2):
        sl = slice(hh * TH, hh * TH + TH)
        ctx.op(DVE, lambda e, hh=hh, sl=sl: e.tensor_copy(bcA[:, sl], pb[6 + hh][:, :]))
    DVE_RMSBC = cnt("dve")

    for k in range(NDT):
        b = k % 2
        if k >= 2:
            ctx.wait(DVE, f"sao{b}", 16 * ((k - 2) // 2 + 1))
        ctx.op(DVE, lambda e, k=k, b=b: e.scalar_tensor_tensor(
            aosp[:, b, :], iin[:, k, :], cst[:, C_GNW + k:C_GNW + k + 1], bcA[:, :],
            op0=MUL, op1=MUL))
        m[("ao", k)] = cnt("dve")
        ctx.wait(SY, "dve", m[("ao", k)])
        ctx.dma(SY, f"sao{b}", dr["aod"][k * P:(k + 1) * P, :], aosp[:, b, :])
        m[("sao", k)] = cnt(f"sao{b}")
    DVE_GATE = cnt("dve")

    ctx.wait(GP, "dve", DVE_GATE)
    for c in range(2):
        m[("ldn", c)] = ctx.dma(GP, f"ldn{c}", dnb[:, c, :, :], dn_src(c))

    def g_stat(c, cb, d, sub):
        if c == 0:
            return gupre[:, 0, d, sub:sub + P]
        if c == 1:
            return gupre[:, 2, d, sub:sub + P]
        return wgb[:, cb, d, sub:sub + P]

    def u_stat(c, cb, d, sub):
        if c == 0:
            return gupre[:, 1, d, sub:sub + P]
        if c == 1:
            return gupre[:, 3, d, sub:sub + P]
        return wub[:, cb, d, sub:sub + P]

    def gu_section(g):
        hb = g % 2
        for it in range(IG):
            ti = IG * g + it
            c = ti // 2
            sub = (ti % 2) * P
            cb = c % 2
            ctx.wait(PE, f"lwg{cb}", m[("lgu", c)][0])
            ctx.wait(PE, f"lwu{cb}", m[("lgu", c)][1])
            if ti == 0:
                ctx.wait(PE, "dve", m[("tmp", NDT - 2)])
                ctx.wait(PE, "act", m[("gsq", NDT - 2)])
            else:
                ctx.wait(PE, "dve", m[("gsub", ti - 1)])
            for d in range(NDT):
                st, sp = d == 0, d == NDT - 1
                ctx.op(PE, lambda e, d=d, c=c, cb=cb, sub=sub, st=st, sp=sp: e.matmul(
                    pb[0][:, :], g_stat(c, cb, d, sub), xn[:, d, 0:TH],
                    start=st, stop=sp))
                ctx.op(PE, lambda e, d=d, c=c, cb=cb, sub=sub, st=st, sp=sp: e.matmul(
                    pb[1][:, :], g_stat(c, cb, d, sub), xn[:, d, TH:T],
                    start=st, stop=sp), incs=[("pe", 1)] if sp else ())
            m[("gmm", ti)] = cnt("pe")
            if ti == 0:
                ctx.wait(PE, "dve", m[("tmp", NDT - 1)])
                ctx.wait(PE, "act", m[("gsq", NDT - 1)])
            if ti >= 1:
                ctx.wait(PE, "dve", m[("hact", ti - 1)])
            for d in range(NDT):
                st, sp = d == 0, d == NDT - 1
                ctx.op(PE, lambda e, d=d, c=c, cb=cb, sub=sub, st=st, sp=sp: e.matmul(
                    pb[2][:, :], u_stat(c, cb, d, sub), xn[:, d, 0:TH],
                    start=st, stop=sp))
                ctx.op(PE, lambda e, d=d, c=c, cb=cb, sub=sub, st=st, sp=sp: e.matmul(
                    pb[3][:, :], u_stat(c, cb, d, sub), xn[:, d, TH:T],
                    start=st, stop=sp), incs=[("pe", 1)] if sp else ())
            m[("umm", ti)] = cnt("pe")
            if ti % 2 == 1:
                m[("guf", c)] = cnt("pe")

            ctx.wait(DVE, "pe", m[("gmm", ti)])
            if g >= 2 and it == 0:
                ctx.wait(DVE, "pe", m[("dng", g - 2)])
            for hh in range(2):
                sl = slice(hh * TH, hh * TH + TH)
                ctx.op(DVE, lambda e, hh=hh, sl=sl: e.tensor_tensor(
                    scrB[:, sl], pb[hh][:, :], bcB[:, sl], op=ADD))
            m[("gsub", ti)] = cnt("dve")
            ctx.wait(ACT, "dve", m[("gsub", ti)])
            if ti == 0:
                ctx.wait(ACT, "sao0", 16 * (NDT // 2))
                ctx.wait(ACT, "sao1", 16 * (NDT // 2))
            if ti >= 1:
                ctx.wait(ACT, "dve", m[("hact", ti - 1)])
            for hh in range(2):
                ctx.op(ACT, lambda e, ti=ti, hh=hh: e.activation(
                    sgg[:, hh, :], scrB[:, hh * TH:hh * TH + TH], AF.Silu,
                    scale=cst[:, C_GSC + ti:C_GSC + ti + 1]))
            m[("sgg", ti)] = cnt("act")
            ctx.wait(DVE, "pe", m[("umm", ti)])
            ctx.wait(DVE, "act", m[("sgg", ti)])
            for hh in range(2):
                sl = slice(hh * TH, hh * TH + TH)
                ctx.op(DVE, lambda e, hh=hh, sl=sl: e.tensor_tensor(
                    scrA[:, hh, :], pb[2 + hh][:, :], bcB[:, sl], op=ADD))
                ctx.op(DVE, lambda e, ti=ti, it=it, hb=hb, hh=hh: e.scalar_tensor_tensor(
                    hact[:, hb, it, hh * TH:hh * TH + TH], scrA[:, hh, :],
                    cst[:, C_USC + ti:C_USC + ti + 1], sgg[:, hh, :], op0=MUL, op1=MUL))
            m[("hact", ti)] = cnt("dve")

    def dn_section(g):
        hb = g % 2
        ctx.wait(PE, "dve", m[("hact", IG * g + IG - 1)])
        if g == 0:
            ctx.wait(PE, "act", A_RMS)
            ctx.wait(PE, "dve", DVE_RMSBC)
        for it in range(IG):
            ti = IG * g + it
            st, sp = ti == 0, ti == NIT - 1
            ctx.op(PE, lambda e, it=it, hb=hb, st=st, sp=sp: e.matmul(
                pb[6][0:1, :], onecb[:, 0:1], hact[:, hb, it, 0:TH], start=st, stop=sp))
            ctx.op(PE, lambda e, it=it, hb=hb, st=st, sp=sp: e.matmul(
                pb[7][0:1, :], onecb[:, 0:1], hact[:, hb, it, TH:T], start=st, stop=sp),
                incs=[("pe", 1)] if sp else ())
        if g == NG - 1:
            m[("rsf",)] = cnt("pe")
            ctx.wait(DVE, "pe", m[("rsf",)])
            for hh in range(2):
                sl = slice(hh * TH, hh * TH + TH)
                ctx.op(DVE, lambda e, hh=hh, sl=sl: e.tensor_scalar_mul(
                    bcA[0:1, sl], pb[6 + hh][0:1, :], cst[0:1, C_N128:C_N128 + 1]))
            m[("rsrow",)] = cnt("dve")
            ctx.wait(PE, "dve", m[("rsrow",)])
            ctx.wait(PE, "dve", m[("gsub", NIT - 1)])
            for hh in range(2):
                sl = slice(hh * TH, hh * TH + TH)
                ctx.op(PE, lambda e, hh=hh, sl=sl: e.matmul(
                    pb[hh][:, :], onesr[0:1, 0:P], bcA[0:1, sl], start=True, stop=True),
                    incs=[("pe", 1)])
            m[("rsbm",)] = cnt("pe")
            ctx.wait(DVE, "pe", m[("rsbm",)])
            for hh in range(2):
                sl = slice(hh * TH, hh * TH + TH)
                ctx.op(DVE, lambda e, hh=hh, sl=sl: e.tensor_copy(bcA[:, sl],
                                                                 pb[hh][:, :]))
            m[("rsbc",)] = cnt("dve")
        ctx.wait(PE, f"ldn{(2 * g) % 2}", m[("ldn", 2 * g)])
        ctx.wait(PE, f"ldn{(2 * g + 1) % 2}", m[("ldn", 2 * g + 1)])
        for ht in range(NDT):
            for hh in range(2):
                j = 2 * ht + hh
                dpb = 4 + (j % 2)
                if g > 0 or j >= 2:
                    key = ("dnadd", g, j - 2) if j >= 2 else ("dnadd", g - 1, 30 + j)
                    ctx.wait(PE, "dve", m[key])
                for it in range(IG):
                    st, sp = it == 0, it == IG - 1
                    c = (IG * g + it) // 2
                    a = it % 2
                    cb = c % 2
                    ctx.op(PE, lambda e, it=it, a=a, cb=cb, ht=ht, hh=hh, hb=hb, dpb=dpb, st=st, sp=sp: e.matmul(
                        pb[dpb][:, :], dnb[:, cb, a, ht * P:(ht + 1) * P],
                        hact[:, hb, it, hh * TH:hh * TH + TH], start=st, stop=sp),
                        incs=[("pe", 1)] if sp else ())
                m[("dnmm", g, j)] = cnt("pe")
                ctx.wait(DVE, "pe", m[("dnmm", g, j)])
                if g == 0:
                    ctx.op(DVE, lambda e, ht=ht, hh=hh, dpb=dpb: e.tensor_copy(
                        acc[:, ht, hh * TH:hh * TH + TH], pb[dpb][:, :]))
                else:
                    ctx.op(DVE, lambda e, ht=ht, hh=hh, dpb=dpb: e.tensor_tensor(
                        acc[:, ht, hh * TH:hh * TH + TH], pb[dpb][:, :],
                        acc[:, ht, hh * TH:hh * TH + TH], op=ADD))
                m[("dnadd", g, j)] = cnt("dve")
        m[("dng", g)] = cnt("pe")

    for g in range(NG):
        gu_section(g)
        for c in (2 * g + 2, 2 * g + 3):
            if c < 2 * NG:
                cb = c % 2
                ctx.wait(GP, "dve", DVE_GATE)
                ctx.wait(GP, "pe", m[("guf", c - 2)])
                m[("lgu", c)] = (ctx.dma(GP, f"lwg{cb}", wgb[:, cb, :, :],
                                         gu_src(dr["guT"], c)),
                                 ctx.dma(GP, f"lwu{cb}", wub[:, cb, :, :],
                                         gu_src(dr["upT"], c)))
        if g >= 1:
            dn_section(g - 1)
            for c in (2 * g, 2 * g + 1):
                cb = c % 2
                ctx.wait(GP, "pe", m[("dng", g - 1)])
                m[("ldn", c)] = ctx.dma(GP, f"ldn{cb}", dnb[:, cb, :, :], dn_src(c))
    ctx.wait(SY, "sao0", m[("sao", NDT - 2)])
    ctx.wait(SY, "sao1", m[("sao", NDT - 1)])
    ctx.wait(SY, "pe", m[("umm", NIT - 1)])
    aodr = dr["aod"].rearrange("(a p) t -> p a t", p=P)
    for j in (2, 3):
        ctx.dma(SY, "ldao", aosb[:, 4 * j:4 * j + 4, :], aodr[:, 4 * j:4 * j + 4, :])
    ctx.wait(SY, "pe", m[("dng", NG - 2)])
    ctx.dma(SY, "ldao", aosb[:, 0:4, :], aodr[:, 0:4, :])
    LDAO = cnt("ldao")
    ctx.wait(GP, "pe", m[("umm", NIT - 1)])
    for c in range(2):
        m[("lwo", c)] = ctx.dma(GP, f"lwo{c}", wob[:, c, :, :], dr["woT"][c, :, :, :])

    dn_section(NG - 1)
    PE_D = cnt("pe")

    DVE_RSBC2 = m[("rsbc",)]

    ctx.wait(SY, "pe", PE_D)
    ctx.dma(SY, "ldao2", aosb[:, 4:8, :], aodr[:, 4:8, :])
    LDAO2 = cnt("ldao2")
    for k in range(2):
        m[("lxe", k)] = ctx.dma(SY, f"lxe{k}", xeb[:, k, :], dr["xT"][:, k, :])

    for k in range(NDT):
        b = k % 2
        c = k // 2
        cb = c % 2
        sub = (k % 2) * P
        ctx.wait(PE, "ldao", LDAO)
        ctx.wait(PE, "ldao2", LDAO2)
        ctx.wait(PE, f"lwo{cb}", m[("lwo", c)])
        if k < 2:
            ctx.wait(PE, "dve", DVE_RSBC2)
        else:
            ctx.wait(PE, "dve", m[("sfree", k - 2)])
        for hh in range(2):
            spb = 2 * b + hh
            for et in range(NDT):
                st, sp = et == 0, et == NDT - 1
                ctx.op(PE, lambda e, et=et, cb=cb, sub=sub, hh=hh, spb=spb, st=st, sp=sp: e.matmul(
                    pb[spb][:, :], wob[:, cb, et, sub:sub + P],
                    aosb[:, et, hh * TH:hh * TH + TH], start=st, stop=sp),
                    incs=[("pe", 1)] if sp else ())
        m[("smm", k)] = cnt("pe")
        if k % 2 == 1:
            m[("wof", c)] = cnt("pe")

        if k >= 1:
            ctx.wait(DVE, "act", m[("gf", k - 1)])
        ctx.wait(DVE, "pe", PE_D)
        ctx.op(DVE, lambda e, k=k: e.tensor_tensor(scrE[:, :], acc[:, k, :],
                                                   bcA[:, :], op=ADD))
        m[("esub", k)] = cnt("dve")
        ctx.wait(ACT, "dve", m[("esub", k)])
        if k >= 1:
            ctx.wait(ACT, "dve", m[("sfree", k - 1)])
        for hh in range(2):
            sl = slice(hh * TH, hh * TH + TH)
            ctx.op(ACT, lambda e, k=k, hh=hh, sl=sl: e.activation(
                scrA[:, hh, :], scrE[:, sl], AF.Silu,
                scale=cst[:, C_DSC + k:C_DSC + k + 1]))
        m[("gf", k)] = cnt("act")
        ctx.wait(DVE, "pe", m[("smm", k)])
        ctx.wait(DVE, "act", m[("gf", k)])
        ctx.wait(DVE, f"lxe{b}", m[("lxe", k)])
        if k >= 2:
            ctx.wait(DVE, f"sout{b}", m[("out", k - 2)])
        for hh in range(2):
            spb = 2 * b + hh
            sl = slice(hh * TH, hh * TH + TH)
            ctx.op(DVE, lambda e, hh=hh, spb=spb, sl=sl: e.tensor_tensor(
                scrB[:, sl], pb[spb][:, :], scrA[:, hh, :], op=MUL))
        m[("sfree", k)] = cnt("dve")
        ctx.op(DVE, lambda e, k=k, b=b: e.tensor_tensor(
            outb[:, b, :], scrB[:, :], xeb[:, b, :], op=ADD))
        m[("outv", k)] = cnt("dve")

        ctx.wait(SY, "dve", m[("outv", k)])
        ctx.dma(SY, f"sout{b}", dr["outT"][k * P:(k + 1) * P, :], outb[:, b, :])
        m[("out", k)] = cnt(f"sout{b}")
        if k + 2 < NDT:
            ctx.wait(SY, "dve", m[("outv", k)])
            m[("lxe", k + 2)] = ctx.dma(SY, f"lxe{b}", xeb[:, b, :],
                                        dr["xT"][:, k + 2, :])
            if (k + 2) % 2 == 0 and (k + 2) // 2 >= 2:
                c2 = (k + 2) // 2
                ctx.wait(GP, "pe", m[("wof", c2 - 2)])
                m[("lwo", c2)] = ctx.dma(GP, f"lwo{c2 % 2}", wob[:, c2 % 2, :, :],
                                         dr["woT"][c2, :, :, :])

    ctx.wait(SY, "sout0", m[("out", NDT - 2)])
    ctx.wait(SY, "sout1", m[("out", NDT - 1)])


_NC_CACHE = None


def make_in_maps(inputs):
    x = np.asarray(inputs["L0_attn_hidden_states"])

    def tile_wT(Wm):
        return np.ascontiguousarray(
            np.asarray(Wm).T.reshape(NDT, P, NDT, P).transpose(2, 1, 0, 3))

    wiT = tile_wT(inputs["L0_attn_Wi"])
    wfT = tile_wT(inputs["L0_attn_Wf"])
    wgT = tile_wT(inputs["L0_attn_Wg"])
    woT = np.ascontiguousarray(
        np.asarray(inputs["L0_attn_Wo"]).T.reshape(NDT, P, 8, 256).transpose(2, 1, 0, 3))
    guT = np.ascontiguousarray(
        np.asarray(inputs["L0_ffn_gate_w"]).T.reshape(NDT, P, 32, 256).transpose(2, 1, 0, 3))
    upT = np.ascontiguousarray(
        np.asarray(inputs["L0_ffn_up_w"]).T.reshape(NDT, P, 32, 256).transpose(2, 1, 0, 3))
    dnT = np.ascontiguousarray(
        np.asarray(inputs["L0_ffn_down_w"]).T.reshape(32, 2, P, D).transpose(0, 2, 1, 3))
    gnw = np.ascontiguousarray(np.asarray(inputs["L0_attn_g_norm_weight"]).reshape(NDT, P).T)
    gsc = np.ascontiguousarray(np.asarray(inputs["L0_ffn_gate_scale"]).reshape(NIT, P).T)
    usc = np.ascontiguousarray(np.asarray(inputs["L0_ffn_up_scale"]).reshape(NIT, P).T)
    dsc = np.ascontiguousarray(np.asarray(inputs["L0_ffn_down_scale"]).reshape(NDT, P).T)
    onesd = np.ones((1, P), np.float32)

    in_maps = []
    for cix in range(8):
        bb, r = cix // 4, cix % 4
        xTc = np.ascontiguousarray(
            x[bb, r * T:(r + 1) * T, :].T.reshape(NDT, P, T).transpose(1, 0, 2))
        cstv = np.zeros((P, NC_CST), np.float32)
        cstv[:, C_ONE] = 1.0
        cstv[:, C_INV2048] = 1.0 / D
        cstv[:, C_EPS] = EPS
        cstv[:, C_N128] = -128.0
        if r > 0:
            cstv[:, C_SEL + r - 1] = 1.0
        cstv[:, C_GNW:C_GNW + NDT] = gnw
        cstv[:, C_DSC:C_DSC + NDT] = dsc
        cstv[:, C_GSC:C_GSC + NIT] = gsc
        cstv[:, C_USC:C_USC + NIT] = usc
        in_maps.append(dict(xT=xTc, wiT=wiT, wfT=wfT, wgT=wgT, woT=woT,
                            guT=guT, upT=upT, dnT=dnT, cstd=cstv, onesd=onesd))
    return in_maps


def kernel(L0_attn_hidden_states, L0_attn_Wi, L0_attn_Wf, L0_attn_Wg,
           L0_attn_g_norm_weight, L0_attn_Wo, L0_ffn_gate_w, L0_ffn_gate_scale,
           L0_ffn_up_w, L0_ffn_up_scale, L0_ffn_down_w, L0_ffn_down_scale):
    global _NC_CACHE
    if _NC_CACHE is None:
        _NC_CACHE = build_program()
    nc = _NC_CACHE
    in_maps = make_in_maps(dict(
        L0_attn_hidden_states=L0_attn_hidden_states, L0_attn_Wi=L0_attn_Wi,
        L0_attn_Wf=L0_attn_Wf, L0_attn_Wg=L0_attn_Wg,
        L0_attn_g_norm_weight=L0_attn_g_norm_weight, L0_attn_Wo=L0_attn_Wo,
        L0_ffn_gate_w=L0_ffn_gate_w, L0_ffn_gate_scale=L0_ffn_gate_scale,
        L0_ffn_up_w=L0_ffn_up_w, L0_ffn_up_scale=L0_ffn_up_scale,
        L0_ffn_down_w=L0_ffn_down_w, L0_ffn_down_scale=L0_ffn_down_scale))

    res = run_bass_kernel_spmd(nc, in_maps, core_ids=list(range(8)))

    out = np.empty((2, 4096, D), np.float32)
    for cix in range(8):
        bb, r = cix // 4, cix % 4
        out[bb, r * T:(r + 1) * T, :] = res.results[cix]["outT"].T
    return out
